# version 08
# baseline (speedup 1.0000x reference)
import numpy as np
import ml_dtypes

import concourse.bass as bass
import concourse.mybir as mybir
from concourse.bass_utils import run_bass_kernel_spmd

F32 = mybir.dt.float32
BF16 = mybir.dt.bfloat16
AF = mybir.ActivationFunctionType
ALU = mybir.AluOpType
AX = mybir.AxisListType
BFNP = ml_dtypes.bfloat16

NB = 16
NCORES = 8
BC = 2
C = 1024
NH = 16
HD = 64
LAT = 32
FF = 4096
NE = 8
EPS = 1e-5
ISQ = 32 ** -0.5
FSUP = 1024


def _rope_tables(T):
    try:
        import jax
        import jax.numpy as jnp
        cpu = jax.devices("cpu")[0]
        with jax.default_device(cpu):
            inv_freq = 1.0 / 10000.0 ** (jnp.arange(0, LAT, 2, dtype=jnp.float32) / LAT)
            t = jnp.arange(T, dtype=jnp.float32)
            freqs = jnp.outer(t, inv_freq)
            cos = jnp.repeat(jnp.cos(freqs), 2, axis=-1)[:, :LAT]
            sin = jnp.repeat(jnp.sin(freqs), 2, axis=-1)[:, :LAT]
            return np.asarray(cos, np.float32), np.asarray(sin, np.float32)
    except Exception:
        a = np.arange(0, LAT, 2, dtype=np.float32)
        inv_freq = (np.float32(1.0) / np.float32(10000.0) ** (a / np.float32(LAT))).astype(np.float32)
        t = np.arange(T, dtype=np.float32)
        freqs = np.outer(t, inv_freq).astype(np.float32)
        cos = np.repeat(np.cos(freqs).astype(np.float32), 2, axis=-1)[:, :LAT]
        sin = np.repeat(np.sin(freqs).astype(np.float32), 2, axis=-1)[:, :LAT]
        return cos, sin


def make_consts(T):
    cos, sin = _rope_tables(T)
    cos_pack = np.tile(cos.T, (4, 1)).astype(np.float32)
    sgn = np.where(np.arange(LAT) % 2 == 0, -1.0, 1.0).astype(np.float32)
    sin_pack = np.tile((sin * sgn[None, :]).T, (4, 1)).astype(np.float32)
    ident_bf = np.eye(128, dtype=BFNP)
    diagmask = np.triu(np.ones((128, 128), np.float32)).astype(BFNP)
    headones = np.zeros((128, 4), BFNP)
    for hp in range(4):
        headones[32 * hp:32 * hp + 32, hp] = 1
    ones4x128 = np.ones((4, 128), BFNP)
    mask16 = np.zeros((4, 16), np.float32)
    for pk in range(4):
        for j in range(4):
            mask16[j, 4 * pk + j] = 1.0
    sel4 = np.zeros((128, 128), np.float32)
    sel4[32, 0:32] = 1.0
    sel4[64, 32:64] = 1.0
    sel4[96, 64:96] = 1.0
    sel4[0, 96:128] = 1.0
    m01 = np.zeros((128, 32), BFNP)
    for e in range(NE):
        for b in range(2):
            for i in range(16):
                m01[16 * e + i, 16 * b + i] = 1
    p_e = np.zeros((8, 128), BFNP)
    for e in range(NE):
        p_e[e, 16 * e:16 * e + 16] = 1
    sel2 = np.zeros((2, 256), np.float32)
    sel2[0, 0:128] = 1.0
    sel2[1, 128:256] = 1.0
    ones1x128 = np.ones((1, 128), np.float32)
    ident_f32 = np.eye(128, dtype=np.float32)
    return {
        "cos_pack": cos_pack, "sin_pack": sin_pack, "ident_bf": ident_bf,
        "diagmask": diagmask, "headones": headones, "ones4x128": ones4x128,
        "mask16": mask16, "sel4": sel4,
        "m01": m01, "p_e": p_e, "sel2": sel2, "ones1x128": ones1x128,
        "ident_f32": ident_f32,
    }


CONST_SPECS = [
    ("cos_pack", None, F32), ("sin_pack", None, F32),
    ("ident_bf", [128, 128], BF16), ("diagmask", [128, 128], BF16),
    ("headones", [128, 4], BF16), ("ones4x128", [4, 128], BF16),
    ("mask16", [4, 16], F32), ("sel4", [128, 128], F32),
    ("m01", [128, 32], BF16),
    ("p_e", [8, 128], BF16), ("sel2", [2, 256], F32),
    ("ones1x128", [1, 128], F32), ("ident_f32", [128, 128], F32),
]


class Prog:
    ENGS = ("sync", "scalar", "vector", "tensor", "gpsimd")

    def __init__(self, nc):
        self.nc = nc
        self.ops = {e: [] for e in self.ENGS}
        self.sem = {}
        self.cnt = {}
        for name in ("t", "v", "s", "g"):
            self.ensure(name)

    def ensure(self, name):
        if name not in self.sem:
            self.sem[name] = self.nc.alloc_semaphore("sem_" + name)
            self.cnt[name] = 0
        return self.sem[name]

    def on(self, eng, fn):
        self.ops[eng].append(fn)

    def bump(self, name, k=1):
        self.ensure(name)
        self.cnt[name] += k
        return self.cnt[name]

    def now(self, name):
        return self.cnt[name]

    def wait(self, eng, name, val):
        if val and val > 0:
            sem = self.ensure(name)
            self.ops[eng].append(lambda e: e.wait_ge(sem, val))

    def snap(self):
        return dict(self.cnt)

    def barrier(self, snap=None):
        snap = snap or self.snap()
        for eng in self.ENGS:
            for name, val in snap.items():
                self.wait(eng, name, val)

    def emit(self):
        with self.nc.Block() as blk:
            for eng in self.ENGS:
                ops = self.ops[eng]

                def body(e, ops=ops):
                    for fn in ops:
                        fn(e)
                getattr(blk, eng)(body)


def build_nc(T, upto=None):
    TT = T // 128
    NTB = T // 512
    NKT = T // 128
    NSUP = FF // FSUP
    G64 = FSUP // 64
    NFB = FSUP // 128
    TH = max(1, T // 512)
    THW = T // TH

    nc = bass.Bass("TRN2", target_bir_lowering=False)
    p = Prog(nc)
    A = {}
    guards = {}

    def dram(name, shape, dt=F32, kind="ExternalInput"):
        A[name] = nc.dram_tensor(name, shape, dt, kind=kind)

    def sbuf(name, shape, dt):
        A[name] = nc.alloc_sbuf_tensor(name, shape, dt)

    def sbuf_scoped(name, shape, dt):
        g = nc.sbuf_tensor(name, shape, dt)
        A[name] = g.__enter__()
        guards[name] = g

    def sbuf_free(*names):
        for n in names:
            guards[n].__exit__(None, None, None)
            del guards[n]
        p.barrier()

    dram("x", [BC, T, C])
    dram("c_attn_w", [3 * C, C]); dram("c_attn_b", [3 * C])
    dram("q2l_w", [LAT, HD]); dram("q2l_b", [LAT])
    dram("kv2l_w", [LAT, HD]); dram("kv2l_b", [LAT])
    dram("l2o_w", [HD, LAT]); dram("l2o_b", [HD])
    dram("wqk_w", [LAT, LAT]); dram("wqk_b", [LAT])
    dram("cproj_w", [C, C]); dram("cproj_b", [C])
    dram("gate_w1", [C, C]); dram("gate_b1", [C])
    dram("gate_w2", [NE, C]); dram("gate_b2", [NE])
    dram("W1", [FF, C]); dram("b1", [FF])
    dram("W2", [C, FF]); dram("b2", [C])
    dram("TW1", [NE, FF, C]); dram("Tb1", [NE, FF])
    dram("TW2", [NE, C, FF]); dram("Tb2", [NE, C])
    for cname, shp, cdt in CONST_SPECS:
        dram(cname, shp or [128, T], cdt)
    dram("out", [BC, T, C], F32, kind="ExternalOutput")
    dram("xprime", [BC, T, C], F32, kind="Internal")
    dram("ln2scr", [BC, T, C], BF16, kind="Internal")
    dram("ffnacc", [BC, T, C], F32, kind="Internal")

    PSB = [nc.alloc_psum_tensor(f"pb{i}", [128, 512], F32) for i in range(6)]
    psg = {}

    _psn = [0]

    def psum_scoped(name, shape, dt):
        _psn[0] += 1
        g = nc.psum_tensor(f"{name}_{_psn[0]}", shape, dt)
        h = g.__enter__()
        psg[name] = g
        return h

    def psum_free(*names):
        for n in names:
            psg[n].__exit__(None, None, None)
            del psg[n]
        p.barrier()

    psbf_a = psum_scoped("psbf_a", [128, 1024], BF16)
    psbf_b = psum_scoped("psbf_b", [128, 1024], BF16)
    PSY = [PSB[4], PSB[5], None, None]

    for cname, shp, cdt in CONST_SPECS:
        if cname in ("cos_pack", "sin_pack"):
            continue
        sbuf("k_" + cname, shp or [128, T], cdt)
    sbuf("q2l_st", [128, 64], BF16)
    sbuf("kv2l_st", [128, 64], BF16)
    sbuf("wqk_st", [128, 128], BF16)
    sbuf("l2o_st", [128, 64], BF16)
    sbuf("kv2l_wT", [128, 32], BF16)
    sbuf("gw2T", [128, 64], BF16)
    sbuf("cab", [128, 24], F32)
    sbuf("cab_r", [24, 128], F32)
    sbuf("gb1c", [128, 8], F32)
    sbuf("gb1_r", [8, 128], F32)
    sbuf("b1_r", [32, 128], F32)
    sbuf("q2l_bb", [128, 1], F32)
    sbuf("kv2l_bb", [128, 1], F32)
    sbuf("wqk_bb", [128, 1], F32)
    sbuf("l2o_bb", [128, 1], F32)
    sbuf("kvb_row", [1, 32], F32)
    sbuf("kvb_bc", [128, 32], F32)
    sbuf("cpb_row", [1, C], F32)
    sbuf("cpb_bc", [128, C], F32)
    sbuf("gb2c", [8, 1], F32)
    sbuf("b1T", [128, FF // 128], F32)
    sbuf("mb1T", [128, (FF // 128) * 2], F32)
    sbuf("mb2row", [2, C], F32)
    sbuf("mb2bc", [128, 2 * C], F32)
    sbuf("g32", [8, 4], F32)
    sbuf("gbf", [8, 2], BF16)
    sbuf("gcols", [8, 32], BF16)
    sbuf("Gpat", [128, 32], BF16)
    sbuf("negcm", [4, 16], BF16)
    sbuf("negc_bc", [128, 16], F32)
    sbuf("qn2", [4, 4], F32)
    sbuf("kn2", [4, 4], F32)
    sbuf("negc", [4, 4], F32)
    sbuf("lnt", [128, 16], F32)
    sbuf("g_parts", [8, 8], F32)
    sbuf("g_scr", [8, 512], F32)
    sbuf("sw_scr", [128, 192], BF16)
    sbuf("gw2_raw", [8, C], BF16)
    sbuf("ceps", [128, 1], F32)
    sbuf_scoped("k_cos_pack", [128, T], F32)
    sbuf_scoped("k_sin_pack", [128, T], F32)

    def dma(eng, dst, src, inc, slow=False):
        sem = p.ensure(inc)
        p.ensure(inc)
        if slow:
            p.on(eng, lambda e: e.dma_start(dst, src, allow_slow_non_contiguous=True).then_inc(sem, 16))
        else:
            p.on(eng, lambda e: e.dma_start(dst, src).then_inc(sem, 16))
        return p.bump(inc, 16)

    for cname, shp, cdt in CONST_SPECS:
        dma("sync", A["k_" + cname][:], A[cname][:], "sd0")
    dma("sync", A["cab_r"][:], A["c_attn_b"][:].rearrange("(a p) -> a p", p=128), "sd0")
    dma("sync", A["gb1_r"][:], A["gate_b1"][:].rearrange("(a p) -> a p", p=128), "sd0")
    for hp in range(4):
        dma("sync", A["q2l_bb"][32 * hp:32 * hp + 32, :], A["q2l_b"][:].rearrange("(l a) -> l a", a=1), "sd0", slow=True)
        dma("sync", A["kv2l_bb"][32 * hp:32 * hp + 32, :], A["kv2l_b"][:].rearrange("(l a) -> l a", a=1), "sd0", slow=True)
        dma("sync", A["wqk_bb"][32 * hp:32 * hp + 32, :], A["wqk_b"][:].rearrange("(l a) -> l a", a=1), "sd0", slow=True)
    for hh in range(2):
        dma("sync", A["l2o_bb"][64 * hh:64 * hh + 64, :], A["l2o_b"][:].rearrange("(l a) -> l a", a=1), "sd0", slow=True)
    dma("sync", A["gb2c"][:], A["gate_b2"][:].rearrange("(e a) -> e a", a=1), "sd0", slow=True)
    dma("sync", A["b1_r"][:], A["b1"][:].rearrange("(a p) -> a p", p=128), "sd0")
    dma("sync", A["kvb_row"][:], A["kv2l_b"][:].rearrange("(a l) -> a l", a=1), "sd0")
    dma("sync", A["cpb_row"][:], A["cproj_b"][:].rearrange("(a c) -> a c", a=1), "sd0")
    dma("gpsimd", A["sw_scr"][0:32, 0:64], A["q2l_w"][:], "sg0")
    dma("gpsimd", A["sw_scr"][0:32, 64:128], A["kv2l_w"][:], "sg0")
    dma("gpsimd", A["sw_scr"][0:64, 128:160], A["l2o_w"][:], "sg0")
    dma("gpsimd", A["sw_scr"][0:32, 160:192], A["wqk_w"][:], "sg0")
    dma("gpsimd", A["gw2_raw"][:], A["gate_w2"][:], "sg0")
    p0_sd = p.now("sd0")
    p0_sg = p.now("sg0")

    def fn(e):
        e.memset(A["q2l_st"][:], 0)
        e.memset(A["kv2l_st"][:], 0)
        e.memset(A["ceps"][:], EPS)
        e.memset(A["wqk_st"][:], 0).then_inc(p.sem["v"], 1)
    p.on("vector", fn)
    memset_v = p.bump("v")

    p.wait("tensor", "sg0", p0_sg)
    p.wait("tensor", "sd0", p0_sd)

    def fn(e):
        idt = A["k_ident_bf"]
        e.transpose(psbf_a[0:64, 0:32], A["sw_scr"][0:32, 0:64], idt[0:32, 0:32])
        e.transpose(psbf_a[64:128, 32:64], A["sw_scr"][0:32, 0:64], idt[0:32, 0:32], tile_position=(0, 64))
        e.transpose(psbf_a[0:64, 64:96], A["sw_scr"][0:32, 64:128], idt[0:32, 0:32])
        e.transpose(psbf_a[64:128, 96:128], A["sw_scr"][0:32, 64:128], idt[0:32, 0:32], tile_position=(0, 64))
        e.transpose(psbf_a[0:64, 128:160], A["sw_scr"][0:32, 64:128], idt[0:32, 0:32])
        e.transpose(psbf_a[64:128, 160:192], A["sw_scr"][0:32, 64:128], idt[0:32, 0:32], tile_position=(0, 64))
        for gq in range(4):
            e.transpose(psbf_a[32 * gq:32 * gq + 32, 192 + 64 * gq:256 + 64 * gq],
                        A["sw_scr"][0:64, 128:160], idt[0:64, 0:64], tile_position=(0, 32 * gq))
        for gq in range(4):
            e.transpose(psbf_a[32 * gq:32 * gq + 32, 448 + 32 * gq:480 + 32 * gq],
                        A["sw_scr"][0:32, 160:192], idt[0:32, 0:32], tile_position=(0, 32 * gq))
        for j in range(8):
            mm = e.transpose(psbf_b[0:128, 8 * j:8 * j + 8],
                             A["gw2_raw"][0:8, 128 * j:128 * j + 128], idt[0:8, 0:8])
        idf = A["k_ident_f32"]
        e.transpose(PSB[3][0:128, 0:24], A["cab_r"][0:24, :], idf[0:24, 0:24])
        e.transpose(PSB[3][0:128, 24:32], A["gb1_r"][0:8, :], idf[0:8, 0:8])
        mm = e.transpose(PSB[3][0:128, 32:64], A["b1_r"][0:32, :], idf[0:32, 0:32])
        mm.then_inc(p.sem["t"], 1)
    p.on("tensor", fn)
    smallT = p.bump("t")

    p.wait("scalar", "t", smallT)
    p.wait("scalar", "v", memset_v)

    def fn(e):
        e.copy(A["q2l_st"][0:64, 0:32], psbf_a[0:64, 0:32])
        e.copy(A["q2l_st"][64:128, 32:64], psbf_a[64:128, 32:64])
        e.copy(A["kv2l_st"][0:64, 0:32], psbf_a[0:64, 64:96])
        e.copy(A["kv2l_st"][64:128, 32:64], psbf_a[64:128, 96:128])
        e.copy(A["kv2l_wT"][0:64, :], psbf_a[0:64, 128:160])
        e.copy(A["kv2l_wT"][64:128, :], psbf_a[64:128, 160:192])
        for gq in range(4):
            e.copy(A["l2o_st"][32 * gq:32 * gq + 32, :], psbf_a[32 * gq:32 * gq + 32, 192 + 64 * gq:256 + 64 * gq])
            e.copy(A["wqk_st"][32 * gq:32 * gq + 32, 32 * gq:32 * gq + 32],
                   psbf_a[32 * gq:32 * gq + 32, 448 + 32 * gq:480 + 32 * gq])
        e.copy(A["gw2T"][:], psbf_b[0:128, 0:64])
        e.copy(A["cab"][:], PSB[3][0:128, 0:24])
        e.copy(A["gb1c"][:], PSB[3][0:128, 24:32])
        e.copy(A["b1T"][:], PSB[3][0:128, 32:64]).then_inc(p.sem["s"], 1)
    p.on("scalar", fn)
    smallT_s = p.bump("s")

    p.wait("tensor", "s", smallT_s)

    def fn(e):
        e.matmul(PSB[0][0:128, 0:32], A["k_ones1x128"][:], A["kvb_row"][:], start=True, stop=True)
        e.matmul(PSB[1][:], A["k_ones1x128"][:], A["cpb_row"][0:1, 0:512], start=True, stop=True)
        e.matmul(PSB[2][:], A["k_ones1x128"][:], A["cpb_row"][0:1, 512:1024], start=True, stop=True).then_inc(p.sem["t"], 1)
    p.on("tensor", fn)
    bcb_t = p.bump("t")
    p.wait("scalar", "t", bcb_t)

    def fn(e):
        e.copy(A["kvb_bc"][:], PSB[0][0:128, 0:32])
        e.copy(A["cpb_bc"][:, 0:512], PSB[1][:])
        e.copy(A["cpb_bc"][:, 512:1024], PSB[2][:]).then_inc(p.sem["s"], 1)
    p.on("scalar", fn)
    p.bump("s")

    st = {}

    class _Stop(Exception):
        pass

    def _ck(tag):
        if upto == tag:
            raise _Stop()

    def _finish():
        p.barrier()
        p.emit()
        nc.finalize()
        return nc

    def ln_stats(src_ap):
        lt = A["lnt"]
        p.wait("vector", "s", p.now("s"))

        def fn(e, src_ap=src_ap):
            e.drain()
            e.bn_stats(lt[:, 0:6], src_ap[:, 0:512])
            e.bn_stats(lt[:, 6:12], src_ap[:, 512:1024])
            e.drain()
            e.bn_aggr(lt[:, 12:14], lt[:, 0:12]).then_inc(p.sem["v"], 1)
        p.on("vector", fn)
        vv = p.bump("v")
        p.wait("scalar", "v", vv)

        def fn(e):
            e.activation(lt[:, 14:15], lt[:, 13:14], AF.Sqrt, bias=A["ceps"][:], scale=1.0).then_inc(p.sem["s"], 1)
        p.on("scalar", fn)
        sv = p.bump("s")
        p.wait("vector", "s", sv)

        def fn(e):
            e.reciprocal(lt[:, 15:16], lt[:, 14:15])
            e.drain()
            e.tensor_scalar(lt[:, 12:13], lt[:, 12:13], lt[:, 15:16], -1.0,
                            op0=ALU.mult, op1=ALU.mult).then_inc(p.sem["v"], 1)
        p.on("vector", fn)
        vv2 = p.bump("v")
        p.wait("scalar", "v", vv2)
        return vv2

    def attention(b):
        sbuf_scoped(f"yfm{b}", [128, 8 * T], BF16)
        sbuf_scoped(f"qf{b}", [128, 4 * T], BF16)
        sbuf_scoped(f"kl{b}", [128, 4 * T], BF16)
        sbuf_scoped(f"vaug{b}", [128, NH * NKT * 33], BF16)
        sbuf_scoped(f"ln1fm{b}", [128, 8 * T], BF16)
        sbuf_scoped(f"xt{b}", [128, 2 * C], F32)
        sbuf_scoped(f"ln1tm{b}", [128, 2 * C], BF16)
        lnfm = A[f"ln1fm{b}"]
        for tt in range(TT):
            slot = tt % 2
            xt = A[f"xt{b}"][:, C * slot:C * slot + C]
            if tt >= 2:
                p.wait("sync", "s", st[("ln1_use", b, tt - 2)])
            xv = dma("sync", xt, A["x"][b, 128 * tt:128 * tt + 128, :], f"sx{slot}")
            p.wait("vector", f"sx{slot}", xv)
            ln_stats(xt)

            def fn(e, slot=slot, xt=xt):
                lt = A[f"ln1tm{b}"][:, C * slot:C * slot + C]
                e.activation(lt, xt, AF.Identity, bias=A["lnt"][:, 12:13],
                             scale=A["lnt"][:, 15:16]).then_inc(p.sem["s"], 1)
            p.on("scalar", fn)
            lns = p.bump("s")
            st[("ln1_use", b, tt)] = lns
            p.wait("tensor", "s", lns)
            if tt >= 1:
                p.wait("tensor", "v", st[("ln1fm_v", b, tt - 1)])

            def fn(e, slot=slot):
                lt = A[f"ln1tm{b}"][:, C * slot:C * slot + C]
                for j in range(8):
                    mm = e.transpose(psbf_a[:, 128 * j:128 * j + 128], lt[:, 128 * j:128 * j + 128], A["k_ident_bf"][:])
                mm.then_inc(p.sem["t"], 1)
            p.on("tensor", fn)
            trt = p.bump("t")
            st[("ln1tr", b, tt)] = trt
            p.wait("vector", "t", trt)

            def fn(e, tt=tt):
                dst = lnfm[:].rearrange("p (j t) -> p j t", j=8)[:, :, 128 * tt:128 * tt + 128]
                e.tensor_copy(dst, psbf_a[:].rearrange("p (j t) -> p j t", j=8)).then_inc(p.sem["v"], 1)
            p.on("vector", fn)
            st[("ln1fm_v", b, tt)] = p.bump("v")
        ln1_v = st[("ln1fm_v", b, TT - 1)]
        sbuf_free(f"ln1tm{b}", f"xt{b}")
        _ck(f"ln1_{b}")

        sbuf_scoped(f"cawld{b}", [128, 2 * C], BF16)
        sbuf_scoped(f"cawst{b}", [128, 2 * 1024], BF16)
        sbuf_scoped(f"qkv{b}", [128, 2 * T], BF16)
        sbuf_scoped(f"latraw{b}", [128, T], F32)
        sbuf_scoped(f"rsc{b}", [128, 3 * 512], F32)
        sbuf_scoped(f"qrope{b}", [128, T], BF16)
        qkv_t = A[f"qkv{b}"]

        def fn(e):
            va = A[f"vaug{b}"][:].rearrange("p (k c) -> p k c", c=33)
            e.memset(va[:, :, 32:33], 1.0).then_inc(p.sem["v"], 1)
        p.on("vector", fn)
        vones_v = p.bump("v")

        st[("qkv_seq", b)] = 0

        def qkv_chunk(co):
            seq = st[("qkv_seq", b)]
            st[("qkv_seq", b)] = seq + 1
            ld = seq % 2
            ss = seq % 2
            ring = seq % 2
            prev = st.get(("cawT_seq", b, seq - 2))
            if prev:
                p.wait("gpsimd", "t", prev)
            cav = dma("gpsimd", A[f"cawld{b}"][:, C * ld:C * ld + C],
                      A["c_attn_w"][128 * co:128 * co + 128, :], f"scw{ld}")
            p.wait("tensor", f"scw{ld}", cav)
            if seq == 0:
                p.wait("tensor", "v", ln1_v)
            prev = st.get(("cawst_v_seq", b, seq - 1))
            if prev:
                p.wait("tensor", "v", prev)

            def fn(e, ld=ld):
                src = A[f"cawld{b}"][:, C * ld:C * ld + C]
                for j in range(8):
                    mm = e.transpose(psbf_a[:, 128 * j:128 * j + 128], src[:, 128 * j:128 * j + 128], A["k_ident_bf"][:])
                mm.then_inc(p.sem["t"], 1)
            p.on("tensor", fn)
            cawT = p.bump("t")
            st[("cawT_seq", b, seq)] = cawT
            p.wait("vector", "t", cawT)
            prev = st.get(("caw_mm_t_seq", b, seq - 2))
            if prev:
                p.wait("vector", "t", prev)

            def fn(e, ss=ss):
                e.tensor_copy(A[f"cawst{b}"][:, 1024 * ss:1024 * ss + 1024], psbf_a[:, 0:1024]).then_inc(p.sem["v"], 1)
            p.on("vector", fn)
            cawst_v = p.bump("v")
            st[("cawst_v_seq", b, seq)] = cawst_v

            p.wait("tensor", "v", cawst_v)
            for tb in range(NTB):
                bank = PSB[4 + tb % 2]
                prev = st.get(("qkvbank_v", b, tb % 2))
                if prev:
                    p.wait("tensor", "v", prev)
                prev = st.get(("qkv_cons_t", b, ring))
                if prev:
                    p.wait("vector", "t", prev)

                def fn(e, ss=ss, tb=tb, bank=bank):
                    for j in range(8):
                        mm = e.matmul(bank[:],
                                      A[f"cawst{b}"][:, 1024 * ss + 128 * j:1024 * ss + 128 * j + 128],
                                      lnfm[:].rearrange("p (jj t) -> p jj t", jj=8)[:, j, 512 * tb:512 * tb + 512],
                                      start=(j == 0), stop=(j == 7))
                    mm.then_inc(p.sem["t"], 1)
                p.on("tensor", fn)
                mmt = p.bump("t")
                p.wait("vector", "t", mmt)

                def fn(e, ring=ring, tb=tb, co=co, bank=bank):
                    dst = qkv_t[:, T * ring + 512 * tb:T * ring + 512 * tb + 512]
                    e.tensor_scalar(dst, bank[:], A["cab"][:, co:co + 1], None,
                                    op0=ALU.add, op1=ALU.bypass).then_inc(p.sem["v"], 1)
                p.on("vector", fn)
                st[("qkvbank_v", b, tb % 2)] = p.bump("v")
            st[("qkv_ev", b, co)] = p.now("v")
            st[("caw_mm_t_seq", b, seq)] = p.now("t")
            return ring

        def qkv_consumed(*rings):
            for r in rings:
                st[("qkv_cons_t", b, r)] = p.now("t")

        def rope_pack(src, dst):
            sh = A[f"rsc{b}"][:, 0:512]
            t1 = A[f"rsc{b}"][:, 512:1024]
            t2 = A[f"rsc{b}"][:, 1024:1536]
            mask = []
            for i in range(16):
                mask += [2 * i + 1, 2 * i]

            def fn(e):
                for blk in range(NTB):
                    sl = slice(512 * blk, 512 * blk + 512)
                    e.drain()
                    e.stream_shuffle(sh, src[:, sl], mask)
                    e.tensor_mul(t1, src[:, sl], A["k_cos_pack"][:, sl])
                    e.drain()
                    e.tensor_mul(t2, sh, A["k_sin_pack"][:, sl])
                    e.drain()
                    ta = e.tensor_add(dst[:, sl], t1, t2)
                ta.then_inc(p.sem["v"], 1)
            p.on("vector", fn)
            return p.bump("v")

        def latent_mms(r0, r1, stn, bias_name, out_pref):
            for tb in range(NTB):
                bank = PSB[tb % 2]
                prev = st.get(("latbank_s", b, tb % 2))
                if prev:
                    p.wait("tensor", "s", prev)

                def fn(e, r0=r0, r1=r1, tb=tb, bank=bank, stn=stn):
                    e.matmul(bank[0:64, :], A[stn][:, 0:64],
                             qkv_t[:, T * r0 + 512 * tb:T * r0 + 512 * tb + 512],
                             start=True, stop=True)
                    e.matmul(bank[64:128, :], A[stn][:, 0:64],
                             qkv_t[:, T * r1 + 512 * tb:T * r1 + 512 * tb + 512],
                             start=True, stop=True, tile_position=(0, 64)).then_inc(p.sem["t"], 1)
                p.on("tensor", fn)
                mmt = p.bump("t")
                p.wait("scalar", "t", mmt)

                def fn(e, tb=tb, bank=bank, bias_name=bias_name, out_pref=out_pref):
                    e.activation(out_pref[:, 512 * tb:512 * tb + 512], bank[:],
                                 AF.Identity, bias=A[bias_name][:], scale=1.0).then_inc(p.sem["s"], 1)
                p.on("scalar", fn)
                st[("latbank_s", b, tb % 2)] = p.bump("s")
            return p.now("s")

        for pk in range(4):
            r0 = qkv_chunk(2 * pk)
            r1 = qkv_chunk(2 * pk + 1)
            p.wait("tensor", "v", st[("qkv_ev", b, 2 * pk + 1)])
            if pk > 0:
                p.wait("tensor", "v", st[("rope_q", b, pk - 1)])
            qlat_s = latent_mms(r0, r1, "q2l_st", "q2l_bb", A[f"latraw{b}"])
            qkv_consumed(r0, r1)
            if pk == 0:
                _ck(f"qlat0_{b}")
            p.wait("vector", "s", qlat_s)
            rq = rope_pack(A[f"latraw{b}"], A[f"qrope{b}"])
            st[("rope_q", b, pk)] = rq
            if pk == 0:
                _ck(f"rope0_{b}")
            p.wait("tensor", "v", rq)
            for tb in range(NTB):
                bank = PSB[tb % 2]
                prev = st.get(("latbank_s", b, tb % 2))
                if prev:
                    p.wait("tensor", "s", prev)

                def fn(e, tb=tb, bank=bank):
                    e.matmul(bank[:], A["wqk_st"][:],
                             A[f"qrope{b}"][:, 512 * tb:512 * tb + 512],
                             start=True, stop=True).then_inc(p.sem["t"], 1)
                p.on("tensor", fn)
                mmt = p.bump("t")
                p.wait("scalar", "t", mmt)

                def fn(e, tb=tb, bank=bank, pk=pk):
                    e.activation(A[f"qf{b}"][:, T * pk + 512 * tb:T * pk + 512 * tb + 512], bank[:],
                                 AF.Identity, bias=A["wqk_bb"][:], scale=1.0).then_inc(p.sem["s"], 1)
                p.on("scalar", fn)
                st[("latbank_s", b, tb % 2)] = p.bump("s")

            if pk == 0:
                _ck(f"wqk0_{b}")
            r0 = qkv_chunk(8 + 2 * pk)
            r1 = qkv_chunk(8 + 2 * pk + 1)
            p.wait("tensor", "v", st[("qkv_ev", b, 8 + 2 * pk + 1)])
            klat_s = latent_mms(r0, r1, "kv2l_st", "kv2l_bb", A[f"latraw{b}"])
            qkv_consumed(r0, r1)
            p.wait("vector", "s", klat_s)
            rk = rope_pack(A[f"latraw{b}"], A[f"kl{b}"][:, T * pk:T * pk + T])
            st[("rope_k", b, pk)] = rk
            if pk == 0:
                _ck(f"klat0_{b}")

        p.wait("tensor", "v", vones_v)
        for vc in range(8):
            ring = qkv_chunk(16 + vc)
            if vc == 0:
                _ck(f"vchunk_{b}")
            p.wait("tensor", "v", st[("qkv_ev", b, 16 + vc)])
            for tt2 in range(0, TT, 2):
                prev = st.get(("vbank_v", b))
                if prev:
                    p.wait("tensor", "v", prev)

                def fn(e, ring=ring, tt2=tt2):
                    for dt_ in range(min(2, TT - tt2)):
                        tt = tt2 + dt_
                        e.matmul(PSB[2][:, 32 * dt_:32 * dt_ + 32],
                                 qkv_t[0:64, T * ring + 128 * tt:T * ring + 128 * tt + 128],
                                 A["kv2l_wT"][0:64, :], start=True, stop=True)
                        mm = e.matmul(PSB[3][:, 32 * dt_:32 * dt_ + 32],
                                      qkv_t[64:128, T * ring + 128 * tt:T * ring + 128 * tt + 128],
                                      A["kv2l_wT"][64:128, :], start=True, stop=True, tile_position=(64, 0))
                    mm.then_inc(p.sem["t"], 1)
                p.on("tensor", fn)
                mmt = p.bump("t")
                if vc == 0 and tt2 == 0:
                    _ck(f"vmm_{b}")
                p.wait("vector", "t", mmt)

                def fn(e, vc=vc, tt2=tt2):
                    va = A[f"vaug{b}"][:]
                    nd = min(2, TT - tt2)
                    for hh in range(2):
                        h = 2 * vc + hh
                        dst = va[:, (h * NKT + tt2) * 33:(h * NKT + tt2) * 33 + 33 * nd].rearrange(
                            "p (k c) -> p k c", c=33)[:, :, 0:32]
                        src = PSB[2 + hh][:, 0:32 * nd].rearrange("p (k c) -> p k c", c=32)
                        sv_ = e.scalar_tensor_tensor(dst, src, 1.0, A["kvb_bc"][:].rearrange(
                            "p (k c) -> p k c", k=1).broadcast_to([128, nd, 32]),
                            op0=ALU.mult, op1=ALU.add)
                    sv_.then_inc(p.sem["v"], 1)
                p.on("vector", fn)
                st[("vbank_v", b)] = p.bump("v")
            qkv_consumed(ring)
            if vc == 0:
                _ck(f"v0_{b}")

        sbuf_free(f"qrope{b}", f"rsc{b}", f"latraw{b}", f"qkv{b}",
                  f"cawst{b}", f"cawld{b}", f"ln1fm{b}")
        _ck(f"qkv_{b}")

        sbuf_scoped(f"sqb{b}", [128, T], BF16)
        for which, srcn, dstn in (("q", f"qf{b}", "qn2"), ("k", f"kl{b}", "kn2")):
            for pk in range(4):
                dep_s = st.get(("sq_nm_t", b))
                if dep_s:
                    p.wait("scalar", "t", dep_s)
                if which == "k":
                    p.wait("scalar", "v", st[("rope_k", b, pk)])

                def fn(e, srcn=srcn, pk=pk):
                    e.activation(A[f"sqb{b}"][:], A[srcn][:, T * pk:T * pk + T], AF.Square).then_inc(p.sem["s"], 1)
                p.on("scalar", fn)
                sq_s = p.bump("s")
                p.wait("tensor", "s", sq_s)
                dep_v = st.get(("nm_red_v", b))
                if dep_v:
                    p.wait("tensor", "v", dep_v)

                def fn(e):
                    for tb in range(NTB):
                        mm = e.matmul(PSB[tb % 4][0:4, :], A["k_headones"][:],
                                      A[f"sqb{b}"][:, 512 * tb:512 * tb + 512], start=True, stop=True)
                    mm.then_inc(p.sem["t"], 1)
                p.on("tensor", fn)
                nm_t = p.bump("t")
                st[("sq_nm_t", b)] = nm_t
                p.wait("vector", "t", nm_t)

                def fn(e, pk=pk, dstn=dstn):
                    for tb in range(NTB):
                        e.reduce_max(A["lnt"][0:4, tb:tb + 1], PSB[tb % 4][0:4, :], axis=AX.X)
                    e.drain()
                    e.reduce_max(A[dstn][0:4, pk:pk + 1], A["lnt"][0:4, 0:NTB], axis=AX.X).then_inc(p.sem["v"], 1)
                p.on("vector", fn)
                st[("nm_red_v", b)] = p.bump("v")
        norms_v = p.now("v")
        sbuf_free(f"sqb{b}")
        p.wait("scalar", "v", norms_v)

        def fn(e):
            e.activation(A["qn2"][0:4, :], A["qn2"][0:4, :], AF.Sqrt)
            e.activation(A["kn2"][0:4, :], A["kn2"][0:4, :], AF.Sqrt).then_inc(p.sem["s"], 1)
        p.on("scalar", fn)
        sq2 = p.bump("s")
        p.wait("vector", "s", sq2)

        def fn(e):
            e.tensor_mul(A["negc"][0:4, :], A["qn2"][0:4, :], A["kn2"][0:4, :])
            e.drain()
            e.tensor_scalar(A["negc"][0:4, :], A["negc"][0:4, :], -float(ISQ), None,
                            op0=ALU.mult, op1=ALU.bypass)
            e.drain()
            nc3 = A["negc"][0:4, :].rearrange("p (k o) -> p k o", o=1).broadcast_to([4, 4, 4])
            e.tensor_mul(A["negcm"][0:4, :].rearrange("p (k o) -> p k o", o=4), nc3,
                         A["k_mask16"][0:4, :].rearrange("p (k o) -> p k o", o=4)).then_inc(p.sem["v"], 1)
        p.on("vector", fn)
        negc_v = p.bump("v")
        p.wait("tensor", "v", negc_v)

        def fn(e):
            e.matmul(PSB[0][0:128, 0:16], A["k_ones4x128"][0:4, :], A["negcm"][0:4, :],
                     start=True, stop=True).then_inc(p.sem["t"], 1)
        p.on("tensor", fn)
        negc_t = p.bump("t")
        p.wait("scalar", "t", negc_t)

        def fn(e):
            e.copy(A["negc_bc"][:], PSB[0][0:128, 0:16]).then_inc(p.sem["s"], 1)
        p.on("scalar", fn)
        negc_s = p.bump("s")
        _ck(f"ch_{b}")

        nonlocal psbf_a, psbf_b
        psum_free("psbf_b", "psbf_a")
        PSY[2] = psum_scoped("pb6", [128, 512], F32)
        PSY[3] = psum_scoped("pb7", [128, 512], F32)
        sbuf_scoped(f"et{b}", [128, 8 * 512], BF16)
        sbuf_scoped(f"ysb{b}", [128, T], BF16)
        sbuf_scoped(f"yn{b}", [128, 512], BF16)
        sbuf_scoped(f"Dt{b}", [128, T], F32)

        def fn(e):
            e.memset(A[f"Dt{b}"][:], 1.0).then_inc(p.sem["v"], 1)
        p.on("vector", fn)
        dt_v = p.bump("v")
        p.wait("tensor", "v", dt_v)
        p.wait("tensor", "s", negc_s)
        p.wait("scalar", "s", negc_s)

        for pk in range(4):
            prev = st.get(("selmm_t", b))
            if prev:
                p.wait("vector", "t", prev)
            for qb in range(NTB):
                nkt_q = 4 * qb + 4
                for kt in range(nkt_q):
                    d = kt - 4 * qb
                    q0 = 512 * qb + (128 * d if d >= 0 else 0)
                    nq = 512 * (qb + 1) - q0
                    ets = (kt % 2) * 4
                    for half in (0, 1):
                        rnd = 2 * kt + half
                        bset = 2 * (rnd % 2)
                        prev = st.get(("exp_s", b, pk, rnd % 2))
                        if prev:
                            p.wait("tensor", "s", prev)

                        def fn(e, pk=pk, kt=kt, q0=q0, nq=nq, half=half, bset=bset):
                            for i in range(2):
                                hp = 2 * half + i
                                mm = e.matmul(PSB[bset + i][0:128, 0:nq],
                                              A[f"kl{b}"][32 * hp:32 * hp + 32, T * pk + 128 * kt:T * pk + 128 * kt + 128],
                                              A[f"qf{b}"][32 * hp:32 * hp + 32, T * pk + q0:T * pk + q0 + nq],
                                              start=True, stop=True, tile_position=(32 * hp, 0))
                            mm.then_inc(p.sem["t"], 1)
                        p.on("tensor", fn)
                        s_t = p.bump("t")
                        p.wait("scalar", "t", s_t)
                        prev = st.get(("pv_t", b, pk, kt % 2, half))
                        if prev:
                            p.wait("scalar", "t", prev)
                        prev = st.get(("mask_v", b, pk, kt % 2, half))
                        if prev:
                            p.wait("scalar", "v", prev)

                        def fn(e, pk=pk, ets=ets, nq=nq, half=half, bset=bset):
                            for i in range(2):
                                hp = 2 * half + i
                                h = 4 * pk + hp
                                e.activation(A[f"et{b}"][:, 512 * (ets + hp):512 * (ets + hp) + nq],
                                             PSB[bset + i][0:128, 0:nq], AF.Exp,
                                             bias=A["negc_bc"][:, h:h + 1], scale=ISQ).then_inc(p.sem["s"], 1)
                        p.on("scalar", fn)
                        for _ in range(2):
                            p.bump("s")
                        exp_s = p.now("s")
                        st[("exp_s", b, pk, rnd % 2)] = exp_s
                        if d >= 0:
                            p.wait("vector", "s", exp_s)

                            def fn(e, ets=ets, half=half):
                                for i in range(2):
                                    hp = 2 * half + i
                                    sl = slice(512 * (ets + hp), 512 * (ets + hp) + 128)
                                    mmv = e.tensor_mul(A[f"et{b}"][:, sl], A[f"et{b}"][:, sl], A["k_diagmask"][:])
                                mmv.then_inc(p.sem["v"], 1)
                            p.on("vector", fn)
                            mask_v = p.bump("v")
                            st[("mask_v", b, pk, kt % 2, half)] = mask_v
                            p.wait("tensor", "v", mask_v)
                        else:
                            p.wait("tensor", "s", exp_s)

                        def fn(e, pk=pk, kt=kt, qb=qb, q0=q0, nq=nq, ets=ets, nkt_q=nkt_q, half=half):
                            for i in range(2):
                                hp = 2 * half + i
                                h = 4 * pk + hp
                                mm = e.matmul(PSY[hp][0:33, (q0 - 512 * qb):(q0 - 512 * qb) + nq],
                                              A[f"vaug{b}"][:, (h * NKT + kt) * 33:(h * NKT + kt) * 33 + 33],
                                              A[f"et{b}"][:, 512 * (ets + hp):512 * (ets + hp) + nq],
                                              start=(kt == 0), stop=(kt == nkt_q - 1))
                            mm.then_inc(p.sem["t"], 1)
                        p.on("tensor", fn)
                        pv_t = p.bump("t")
                        st[("pv_t", b, pk, kt % 2, half)] = pv_t
                p.wait("vector", "t", pv_t)
                p.wait("scalar", "t", pv_t)

                def fn(e, pk=pk, qb=qb):
                    e.tensor_copy(A[f"Dt{b}"][32:33, 512 * qb:512 * qb + 512], PSY[0][32:33, :])
                    e.tensor_copy(A[f"Dt{b}"][64:65, 512 * qb:512 * qb + 512], PSY[1][32:33, :])
                    e.tensor_copy(A[f"Dt{b}"][96:97, 512 * qb:512 * qb + 512], PSY[2][32:33, :])
                    e.tensor_copy(A[f"Dt{b}"][0:1, 512 * qb:512 * qb + 512], PSY[3][32:33, :])
                    e.tensor_copy(A[f"ysb{b}"][0:32, 512 * qb:512 * qb + 512], PSY[0][0:32, :])
                    e.tensor_copy(A[f"ysb{b}"][32:64, 512 * qb:512 * qb + 512], PSY[1][0:32, :]).then_inc(p.sem["v"], 1)
                p.on("vector", fn)
                dcp_v = p.bump("v")

                def fn(e, pk=pk, qb=qb):
                    e.copy(A[f"ysb{b}"][64:96, 512 * qb:512 * qb + 512], PSY[2][0:32, :])
                    e.copy(A[f"ysb{b}"][96:128, 512 * qb:512 * qb + 512], PSY[3][0:32, :]).then_inc(p.sem["s"], 1)
                p.on("scalar", fn)
                ycp_s = p.bump("s")
                st[("yev", b, pk, qb)] = (dcp_v, ycp_s)
                p.wait("tensor", "v", dcp_v)
                p.wait("tensor", "s", ycp_s)
            p.wait("vector", "s", p.now("s"))
            p.wait("tensor", "v", p.now("v"))
            for qb in range(NTB):
                prev = st.get(("ynorm_v", b))
                if prev:
                    p.wait("tensor", "v", prev)

                def fn(e, qb=qb):
                    e.matmul(PSB[2][:], A["k_sel4"][:], A[f"Dt{b}"][:, 512 * qb:512 * qb + 512],
                             start=True, stop=True).then_inc(p.sem["t"], 1)
                p.on("tensor", fn)
                sel_t = p.bump("t")
                st[("selmm_t", b)] = sel_t
                p.wait("vector", "t", sel_t)

                def fn(e, qb=qb):
                    e.reciprocal(PSB[2][:], PSB[2][:])
                    e.drain()
                    e.tensor_mul(A[f"yn{b}"][:], A[f"ysb{b}"][:, 512 * qb:512 * qb + 512], PSB[2][:]).then_inc(p.sem["v"], 1)
                p.on("vector", fn)
                yn_v = p.bump("v")
                st[("ynorm_v", b)] = yn_v
                p.wait("tensor", "v", yn_v)
                prev = st.get(("yhead_s", b))
                if prev:
                    p.wait("tensor", "s", prev)

                def fn(e):
                    for hp in range(4):
                        ybank = PSB[4 + hp // 2]
                        mm = e.matmul(ybank[64 * (hp % 2):64 * (hp % 2) + 64, :],
                                      A["l2o_st"][32 * hp:32 * hp + 32, :],
                                      A[f"yn{b}"][32 * hp:32 * hp + 32, :],
                                      start=True, stop=True,
                                      tile_position=(32 * hp, 64 * (hp % 2)))
                    mm.then_inc(p.sem["t"], 1)
                p.on("tensor", fn)
                l2o_t = p.bump("t")
                p.wait("scalar", "t", l2o_t)

                def fn(e, pk=pk, qb=qb):
                    e.activation(A[f"yfm{b}"][:, (2 * pk) * T + 512 * qb:(2 * pk) * T + 512 * qb + 512],
                                 PSB[4][:], AF.Identity, bias=A["l2o_bb"][:], scale=1.0)
                    e.activation(A[f"yfm{b}"][:, (2 * pk + 1) * T + 512 * qb:(2 * pk + 1) * T + 512 * qb + 512],
                                 PSB[5][:], AF.Identity, bias=A["l2o_bb"][:], scale=1.0).then_inc(p.sem["s"], 1)
                p.on("scalar", fn)
                st[("yhead_s", b)] = p.bump("s")
        psum_free("pb7", "pb6")
        psbf_a = psum_scoped("psbf_a", [128, 1024], BF16)
        psbf_b = psum_scoped("psbf_b", [128, 1024], BF16)
        yfm_s = p.now("s")

        sbuf_free(f"Dt{b}", f"yn{b}", f"ysb{b}", f"et{b}", f"vaug{b}", f"kl{b}", f"qf{b}")
        _ck(f"sv_{b}")

        sbuf_scoped(f"cpld{b}", [128, 2 * C], BF16)
        sbuf_scoped(f"cpT{b}", [128, 8 * C], BF16)
        sbuf_scoped(f"attnf{b}", [128, 2 * C], F32)
        sbuf_scoped(f"xt2{b}", [128, 2 * C], F32)
        sbuf_scoped(f"xtl{b}", [128, 2 * C], F32)
        sbuf_scoped(f"xp{b}", [128, 2 * C], F32)
        sbuf_scoped(f"ln2tm{b}", [128, 2 * C], BF16)
        for cb in range(8):
            ld = cb % 2
            if cb >= 2:
                p.wait("gpsimd", "t", st[("cpT_t", b, cb - 2)])
            cpv = dma("gpsimd", A[f"cpld{b}"][:, C * ld:C * ld + C],
                      A["cproj_w"][128 * cb:128 * cb + 128, :], f"scp{ld}")
            p.wait("tensor", f"scp{ld}", cpv)
            if cb >= 1:
                p.wait("tensor", "v", st[("cpT_v", b, cb - 1)])

            def fn(e, ld=ld):
                src = A[f"cpld{b}"][:, C * ld:C * ld + C]
                for j in range(8):
                    mm = e.transpose(psbf_a[:, 128 * j:128 * j + 128], src[:, 128 * j:128 * j + 128], A["k_ident_bf"][:])
                mm.then_inc(p.sem["t"], 1)
            p.on("tensor", fn)
            st[("cpT_t", b, cb)] = p.bump("t")
            p.wait("vector", "t", st[("cpT_t", b, cb)])

            def fn(e, cb=cb):
                dst = A[f"cpT{b}"][:].rearrange("p (j c) -> p j c", j=8)[:, :, 128 * cb:128 * cb + 128]
                e.tensor_copy(dst, psbf_a[:].rearrange("p (j c) -> p j c", j=8)).then_inc(p.sem["v"], 1)
            p.on("vector", fn)
            st[("cpT_v", b, cb)] = p.bump("v")
        cpT_v = st[("cpT_v", b, 7)]
        p.wait("tensor", "v", cpT_v)
        p.wait("tensor", "s", yfm_s)

        for tt in range(TT):
            prev = st.get(("site1_v", b))
            if prev:
                p.wait("tensor", "v", prev)

            def fn(e, tt=tt):
                for j in range(8):
                    lhs = A[f"yfm{b}"][:, j * T + 128 * tt:j * T + 128 * tt + 128]
                    e.matmul(PSB[4][:], lhs, A[f"cpT{b}"][:, j * C:j * C + 512],
                             start=(j == 0), stop=(j == 7))
                    mm = e.matmul(PSB[5][:], lhs, A[f"cpT{b}"][:, j * C + 512:j * C + 1024],
                                  start=(j == 0), stop=(j == 7))
                mm.then_inc(p.sem["t"], 1)
            p.on("tensor", fn)
            cp_t = p.bump("t")
            slot = tt % 2
            af = A[f"attnf{b}"][:, C * slot:C * slot + C]
            p.wait("vector", "t", cp_t)

            def fn(e, af=af):
                e.tensor_add(af[:, 0:512], PSB[4][:], A["cpb_bc"][:, 0:512])
                e.tensor_add(af[:, 512:1024], PSB[5][:], A["cpb_bc"][:, 512:1024]).then_inc(p.sem["v"], 1)
            p.on("vector", fn)
            af_v = p.bump("v")
            st[("site1_v", b)] = af_v
            ln_stats(af)
            xr = A[f"xt2{b}"][:, C * slot:C * slot + C]
            if tt >= 2:
                p.wait("sync", "s", st[("x2_use", b, tt - 2)])
            xv = dma("sync", xr, A["x"][b, 128 * tt:128 * tt + 128, :], f"sxr{slot}")
            p.wait("scalar", f"sxr{slot}", xv)
            if tt >= 2:
                p.wait("scalar", "v", st[("xp_v", b, tt - 2)])

            def fn(e, slot=slot, xr=xr):
                e.activation(A[f"xtl{b}"][:, C * slot:C * slot + C], xr, AF.Identity,
                             bias=A["lnt"][:, 12:13], scale=1.0).then_inc(p.sem["s"], 1)
            p.on("scalar", fn)
            xtl_s = p.bump("s")
            st[("x2_use", b, tt)] = xtl_s
            p.wait("vector", "s", xtl_s)
            if tt >= 2:
                p.wait("vector", f"sxp{slot}", st[("xp_dma", b, tt - 2)])
                p.wait("vector", "s", st[("ln2_s", b, tt - 2)])

            def fn(e, slot=slot, af=af):
                e.scalar_tensor_tensor(A[f"xp{b}"][:, C * slot:C * slot + C], af,
                                       A["lnt"][:, 15:16], A[f"xtl{b}"][:, C * slot:C * slot + C],
                                       op0=ALU.mult, op1=ALU.add).then_inc(p.sem["v"], 1)
            p.on("vector", fn)
            xp_v = p.bump("v")
            st[("xp_v", b, tt)] = xp_v
            p.wait("sync", "v", xp_v)
            st[("xp_dma", b, tt)] = dma("sync", A["xprime"][b, 128 * tt:128 * tt + 128, :],
                                        A[f"xp{b}"][:, C * slot:C * slot + C], f"sxp{slot}")
            ln_stats(A[f"xp{b}"][:, C * slot:C * slot + C])
            if tt >= 2:
                p.wait("scalar", f"sl2{slot}", st[("ln2_dma", b, tt - 2)])

            def fn(e, slot=slot):
                e.activation(A[f"ln2tm{b}"][:, C * slot:C * slot + C],
                             A[f"xp{b}"][:, C * slot:C * slot + C], AF.Identity,
                             bias=A["lnt"][:, 12:13], scale=A["lnt"][:, 15:16]).then_inc(p.sem["s"], 1)
            p.on("scalar", fn)
            ln2_s = p.bump("s")
            st[("ln2_s", b, tt)] = ln2_s
            p.wait("sync", "s", ln2_s)
            st[("ln2_dma", b, tt)] = dma("sync", A["ln2scr"][b, 128 * tt:128 * tt + 128, :],
                                         A[f"ln2tm{b}"][:, C * slot:C * slot + C], f"sl2{slot}")
        site1_end = p.snap()
        sbuf_free(f"ln2tm{b}", f"xp{b}", f"xtl{b}", f"xt2{b}", f"attnf{b}",
                  f"cpT{b}", f"cpld{b}", f"yfm{b}")
        return site1_end

    def gate(b):
        sbuf_scoped(f"g_ln2ld{b}", [128, 2 * C], BF16)
        sbuf_scoped(f"g_ln2fm{b}", [128, 8 * T], BF16)
        sbuf_scoped(f"gw1ld{b}", [128, 2 * C], BF16)
        sbuf_scoped(f"gw1T{b}", [128, 2 * 1024], BF16)
        sbuf_scoped(f"hfm{b}", [128, 8 * T], BF16)
        lnf = A[f"g_ln2fm{b}"]
        p.wait("sync", "sl20", p.now("sl20"))
        p.wait("sync", "sl21", p.now("sl21"))
        for tt in range(TT):
            slot = tt % 2
            if tt >= 2:
                p.wait("sync", "t", st[("g_lntr", b, tt - 2)])
            lv = dma("sync", A[f"g_ln2ld{b}"][:, C * slot:C * slot + C],
                     A["ln2scr"][b, 128 * tt:128 * tt + 128, :], f"sgl{slot}")
            p.wait("tensor", f"sgl{slot}", lv)
            if tt >= 1:
                p.wait("tensor", "v", st[("g_lnv", b, tt - 1)])

            def fn(e, slot=slot):
                src = A[f"g_ln2ld{b}"][:, C * slot:C * slot + C]
                for j in range(8):
                    mm = e.transpose(psbf_a[:, 128 * j:128 * j + 128], src[:, 128 * j:128 * j + 128], A["k_ident_bf"][:])
                mm.then_inc(p.sem["t"], 1)
            p.on("tensor", fn)
            st[("g_lntr", b, tt)] = p.bump("t")
            p.wait("vector", "t", st[("g_lntr", b, tt)])

            def fn(e, tt=tt):
                dst = lnf[:].rearrange("p (j t) -> p j t", j=8)[:, :, 128 * tt:128 * tt + 128]
                e.tensor_copy(dst, psbf_a[:].rearrange("p (j t) -> p j t", j=8)).then_inc(p.sem["v"], 1)
            p.on("vector", fn)
            st[("g_lnv", b, tt)] = p.bump("v")
        lnf_v = st[("g_lnv", b, TT - 1)]
        p.wait("tensor", "v", lnf_v)

        for oc in range(8):
            ld = oc % 2
            if oc >= 2:
                p.wait("gpsimd", "t", st[("gw1_t", b, oc - 2)])
            gv = dma("gpsimd", A[f"gw1ld{b}"][:, C * ld:C * ld + C],
                     A["gate_w1"][128 * oc:128 * oc + 128, :], f"sgw{ld}")
            p.wait("tensor", f"sgw{ld}", gv)
            if oc >= 1:
                p.wait("tensor", "v", st[("gw1_v", b, oc - 1)])

            def fn(e, ld=ld):
                src = A[f"gw1ld{b}"][:, C * ld:C * ld + C]
                for j in range(8):
                    mm = e.transpose(psbf_a[:, 128 * j:128 * j + 128], src[:, 128 * j:128 * j + 128], A["k_ident_bf"][:])
                mm.then_inc(p.sem["t"], 1)
            p.on("tensor", fn)
            st[("gw1_t", b, oc)] = p.bump("t")
            p.wait("vector", "t", st[("gw1_t", b, oc)])

            def fn(e, ld=ld):
                e.tensor_copy(A[f"gw1T{b}"][:, 1024 * ld:1024 * ld + 1024], psbf_a[:, 0:1024]).then_inc(p.sem["v"], 1)
            p.on("vector", fn)
            st[("gw1_v", b, oc)] = p.bump("v")
            p.wait("tensor", "v", st[("gw1_v", b, oc)])
            for tb in range(NTB):
                bank = PSB[4 + tb % 2]
                prev = st.get(("g1bank_s", b, tb % 2))
                if prev:
                    p.wait("tensor", "s", prev)

                def fn(e, ld=ld, tb=tb, bank=bank):
                    for j in range(8):
                        mm = e.matmul(bank[:],
                                      A[f"gw1T{b}"][:, 1024 * ld + 128 * j:1024 * ld + 128 * j + 128],
                                      lnf[:].rearrange("p (jj t) -> p jj t", jj=8)[:, j, 512 * tb:512 * tb + 512],
                                      start=(j == 0), stop=(j == 7))
                    mm.then_inc(p.sem["t"], 1)
                p.on("tensor", fn)
                g1_t = p.bump("t")
                p.wait("scalar", "t", g1_t)

                def fn(e, oc=oc, tb=tb, bank=bank):
                    e.activation(A[f"hfm{b}"][:, oc * T + 512 * tb:oc * T + 512 * tb + 512], bank[:],
                                 AF.Relu, bias=A["gb1c"][:, oc:oc + 1], scale=1.0).then_inc(p.sem["s"], 1)
                p.on("scalar", fn)
                st[("g1bank_s", b, tb % 2)] = p.bump("s")
        h_s = p.now("s")
        p.wait("tensor", "s", h_s)

        for tb in range(NTB):
            def fn(e, tb=tb):
                for j in range(8):
                    mm = e.matmul(PSB[0][0:8, :], A["gw2T"][:, 8 * j:8 * j + 8],
                                  A[f"hfm{b}"][:, j * T + 512 * tb:j * T + 512 * tb + 512],
                                  start=(j == 0), stop=(j == 7))
                mm.then_inc(p.sem["t"], 1)
            p.on("tensor", fn)
            g2_t = p.bump("t")
            p.wait("scalar", "t", g2_t)

            def fn(e, tb=tb):
                e.activation(A["g_scr"][0:8, :], PSB[0][0:8, :], AF.Identity,
                             accum_out=A["g_parts"][0:8, tb:tb + 1]).then_inc(p.sem["s"], 1)
            p.on("scalar", fn)
            g2_s = p.bump("s")
            p.wait("tensor", "s", g2_s)
        g2_all = p.now("s")
        p.wait("vector", "s", g2_all)

        def fn(e, b=b):
            e.reduce_sum(A["g32"][0:8, 2:3], A["g_parts"][0:8, 0:NTB], axis=AX.X)
            e.drain()
            e.tensor_scalar(A["g32"][0:8, b:b + 1], A["g32"][0:8, 2:3], 1.0 / T, A["gb2c"][0:8, :],
                            op0=ALU.mult, op1=ALU.add).then_inc(p.sem["v"], 1)
        p.on("vector", fn)
        gv_v = p.bump("v")
        p.wait("scalar", "v", gv_v)

        def fn(e, b=b):
            e.copy(A["gbf"][0:8, b:b + 1], A["g32"][0:8, b:b + 1]).then_inc(p.sem["s"], 1)
        p.on("scalar", fn)
        g_s = p.bump("s")
        sbuf_free(f"hfm{b}", f"gw1T{b}", f"gw1ld{b}", f"g_ln2fm{b}", f"g_ln2ld{b}")
        return g_s

    if upto == "p0":
        return _finish()
    try:
        attention(0)
    except _Stop:
        return _finish()
    p.barrier()
    if upto == "attn0":
        return _finish()
    try:
        gate(0)
    except _Stop:
        return _finish()
    p.barrier()
    if upto == "gate0":
        return _finish()
    try:
        attention(1)
    except _Stop:
        return _finish()
    p.barrier()
    g_done = gate(1)
    p.barrier()
    if upto == "gate1":
        return _finish()
    sbuf_free("k_sin_pack", "k_cos_pack")

    sbuf_scoped("tb1", [8, FF], BF16)
    sbuf_scoped("tb2", [8, C], BF16)
    sbuf_scoped("b2r", [2, C], F32)
    tbv = dma("gpsimd", A["tb1"][:], A["Tb1"][:], "sdmg")
    tbv = dma("gpsimd", A["tb2"][:], A["Tb2"][:], "sdmg")
    dma("sync", A["b2r"][0:1, :], A["b2"][:].rearrange("(a c) -> a c", a=1), "sdm")
    b2v = dma("sync", A["b2r"][1:2, :], A["b2"][:].rearrange("(a c) -> a c", a=1), "sdm")
    p.wait("tensor", "sdmg", tbv)
    p.wait("vector", "sdm", b2v)
    p.wait("scalar", "s", g_done)

    def fn(e):
        for j in range(16):
            cp = e.copy(A["gcols"][0:8, :].rearrange("p (o c) -> p c o", c=16)[:, j, :], A["gbf"][0:8, :])
        cp.then_inc(p.sem["s"], 1)
    p.on("scalar", fn)
    gcols_s = p.bump("s")
    p.wait("tensor", "s", gcols_s)

    def fn(e):
        e.matmul(PSB[0][0:128, 0:32], A["k_p_e"][0:8, :], A["gcols"][0:8, :], start=True, stop=True)
        for j in range(FF // 128):
            mm = e.matmul(PSB[1][0:128, 2 * j:2 * j + 2], A["tb1"][0:8, 128 * j:128 * j + 128],
                          A["gbf"][0:8, :], start=True, stop=True)
        e.matmul(PSB[2][0:2, :], A["gbf"][0:8, :], A["tb2"][0:8, 0:512], start=True, stop=True)
        mm = e.matmul(PSB[3][0:2, :], A["gbf"][0:8, :], A["tb2"][0:8, 512:1024], start=True, stop=True)
        mm.then_inc(p.sem["t"], 1)
    p.on("tensor", fn)
    mid_t = p.bump("t")
    p.wait("vector", "t", mid_t)

    def fn(e):
        e.tensor_mul(A["Gpat"][:], PSB[0][0:128, 0:32], A["k_m01"][:])
        b1b = A["b1T"][:].rearrange("p (j o) -> p j o", o=1).broadcast_to([128, FF // 128, 2])
        e.tensor_add(A["mb1T"][:].rearrange("p (j o) -> p j o", o=2), PSB[1][0:128, 0:2 * (FF // 128)].rearrange("p (j o) -> p j o", o=2), b1b)
        e.tensor_add(A["mb2row"][0:2, 0:512], PSB[2][0:2, :], A["b2r"][0:2, 0:512])
        e.tensor_add(A["mb2row"][0:2, 512:1024], PSB[3][0:2, :], A["b2r"][0:2, 512:1024]).then_inc(p.sem["v"], 1)
    p.on("vector", fn)
    mid_v = p.bump("v")
    p.wait("tensor", "v", mid_v)

    for b in range(2):
        def fn(e, b=b):
            e.matmul(PSB[4][:], A["k_sel2"][0:2, 128 * b:128 * b + 128],
                     A["mb2row"][0:2, 0:512], start=True, stop=True)
            e.matmul(PSB[5][:], A["k_sel2"][0:2, 128 * b:128 * b + 128],
                     A["mb2row"][0:2, 512:1024], start=True, stop=True).then_inc(p.sem["t"], 1)
        p.on("tensor", fn)
        mb2_t = p.bump("t")
        p.wait("scalar", "t", mb2_t)

        def fn(e, b=b):
            e.copy(A["mb2bc"][:, C * b:C * b + 512], PSB[4][:])
            e.copy(A["mb2bc"][:, C * b + 512:C * b + 1024], PSB[5][:]).then_inc(p.sem["s"], 1)
        p.on("scalar", fn)
        mb2_s = p.bump("s")
        p.wait("tensor", "s", mb2_s)
    p.barrier()
    sbuf_free("b2r", "tb2", "tb1")
    if upto == "mid":
        return _finish()

    sbuf_scoped("ln2fm", [128, 8 * 2 * T], BF16)
    sbuf_scoped("f_ln2ld", [128, 2 * C], BF16)
    _f_lnt = []
    _f_lnv = []
    p.wait("sync", "sl20", p.now("sl20"))
    p.wait("sync", "sl21", p.now("sl21"))
    for b in range(2):
        for tt in range(TT):
            slot = tt % 2
            if len(_f_lnt) >= 2:
                p.wait("sync", "t", _f_lnt[-2])
            lv = dma("sync", A["f_ln2ld"][:, C * slot:C * slot + C],
                     A["ln2scr"][b, 128 * tt:128 * tt + 128, :], f"sfl{slot}")
            p.wait("tensor", f"sfl{slot}", lv)
            if _f_lnv:
                p.wait("tensor", "v", _f_lnv[-1])

            def fn(e, slot=slot):
                src = A["f_ln2ld"][:, C * slot:C * slot + C]
                for j in range(8):
                    mm = e.transpose(psbf_a[:, 128 * j:128 * j + 128], src[:, 128 * j:128 * j + 128], A["k_ident_bf"][:])
                mm.then_inc(p.sem["t"], 1)
            p.on("tensor", fn)
            _f_lnt.append(p.bump("t"))
            p.wait("vector", "t", _f_lnt[-1])

            def fn(e, b=b, tt=tt):
                dst = A["ln2fm"][:].rearrange("p (j b2 t) -> p j b2 t", j=8, b2=2)[:, :, b, 128 * tt:128 * tt + 128]
                e.tensor_copy(dst, psbf_a[:].rearrange("p (j t) -> p j t", j=8)).then_inc(p.sem["v"], 1)
            p.on("vector", fn)
            _f_lnv.append(p.bump("v"))
    p.barrier()
    sbuf_free("f_ln2ld")
    if upto == "ln2fm":
        return _finish()

    sbuf_scoped("twld", [128, 16 * 512], BF16)
    sbuf_scoped("wrep", [128, 3 * 1024], F32)
    sbuf_scoped("mwfm", [128, 2 * 1024], BF16)
    sbuf_scoped("mw1Ts", [128, 8 * G64 * 128], BF16)
    sbuf_scoped("mw2Ts", [128, NFB * 16 * 128], BF16)
    sbuf_scoped("usup", [128, NFB * THW], BF16)
    sbuf_scoped("ffev", [128, 2 * C], F32)

    st["tw_grp"] = 0
    st["wrep_seq"] = 0
    for S in range(NSUP if upto != 'sup1' else 1):
        for mat in (1, 2):
            dst_sup = A["mw1Ts"] if mat == 1 else A["mw2Ts"]
            inner = C if mat == 1 else FSUP
            nib = inner // 512
            for grp in range(G64):
                gseq = st["tw_grp"]
                st["tw_grp"] = gseq + 1
                tbase = (gseq % 2) * 8
                wslot = st["wrep_seq"] % 3
                st["wrep_seq"] += 1
                mslot = grp % 2
                tw_v = None
                for j in range(4):
                    for ib in range(nib):
                        tws = tbase + (j * nib + ib) % 8
                        prev = st.get(("tw_used_t", tws))
                        if prev:
                            p.wait("gpsimd", "t", prev)
                        if mat == 1:
                            src = A["TW1"][:, S * FSUP + grp * 64 + 16 * j:S * FSUP + grp * 64 + 16 * j + 16,
                                           512 * ib:512 * ib + 512]
                        else:
                            src = A["TW2"][:, grp * 64 + 16 * j:grp * 64 + 16 * j + 16,
                                           S * FSUP + 512 * ib:S * FSUP + 512 * ib + 512]
                        tw_v = dma("gpsimd", A["twld"][:, 512 * tws:512 * tws + 512], src, f"stw{gseq % 2}")
                        st[("tw_slot_v", tws)] = tw_v
                prev = st.get(("wrep_used_v", wslot))
                if prev:
                    p.wait("sync", "v", prev)
                wv = None
                for j in range(4):
                    for bb in range(2):
                        if mat == 1:
                            src = A["W1"][S * FSUP + grp * 64 + 16 * j:S * FSUP + grp * 64 + 16 * j + 16, :]
                        else:
                            src = A["W2"][grp * 64 + 16 * j:grp * 64 + 16 * j + 16, S * FSUP:S * FSUP + FSUP]
                        wv = dma("sync", A["wrep"][32 * j + 16 * bb:32 * j + 16 * bb + 16,
                                                   1024 * wslot:1024 * wslot + 1024], src, f"swr{wslot}")
                prev = st.get(("emm_ev_v",))
                if prev:
                    p.wait("tensor", "v", prev)
                p.wait("tensor", f"stw{gseq % 2}", tw_v)

                def fn(e, nib=nib, tbase=tbase):
                    for ib in range(nib):
                        for j in range(4):
                            tws = tbase + (j * nib + ib) % 8
                            mm = e.matmul(PSB[4 + ib % 2][32 * j:32 * j + 32, :], A["Gpat"][:, 0:32],
                                          A["twld"][:, 512 * tws:512 * tws + 512],
                                          start=True, stop=True, tile_position=(0, 32 * j))
                    mm.then_inc(p.sem["t"], 1)
                p.on("tensor", fn)
                emm_t = p.bump("t")
                for k in range(8):
                    st[("tw_used_t", tbase + k)] = emm_t
                p.wait("vector", "t", emm_t)
                p.wait("vector", f"swr{wslot}", wv)
                prev = st.get(("mwfm_used_t", mslot))
                if prev:
                    p.wait("vector", "t", prev)

                def fn(e, nib=nib, wslot=wslot, mslot=mslot):
                    for ib in range(nib):
                        ta = e.tensor_add(A["mwfm"][:, 1024 * mslot + 512 * ib:1024 * mslot + 512 * ib + 512],
                                     PSB[4 + ib % 2][:], A["wrep"][:, 1024 * wslot + 512 * ib:1024 * wslot + 512 * ib + 512])
                    ta.then_inc(p.sem["v"], 1)
                p.on("vector", fn)
                mwfm_v = p.bump("v")
                st[("wrep_used_v", wslot)] = mwfm_v
                st[("emm_ev_v",)] = mwfm_v
                p.wait("tensor", "v", mwfm_v)
                prev = st.get(("mwT_ev_v",))
                if prev:
                    p.wait("tensor", "v", prev)

                def fn(e, mslot=mslot):
                    for sub in range(8):
                        mm = e.transpose(psbf_b[:, 128 * sub:128 * sub + 128],
                                         A["mwfm"][:, 1024 * mslot + 128 * sub:1024 * mslot + 128 * sub + 128],
                                         A["k_ident_bf"][:])
                    mm.then_inc(p.sem["t"], 1)
                p.on("tensor", fn)
                mwT_t = p.bump("t")
                st[("mwfm_used_t", mslot)] = mwT_t
                p.wait("vector", "t", mwT_t)

                def fn(e, grp=grp, dst_sup=dst_sup, mat=mat):
                    srcv = psbf_b[:].rearrange("p (s j bb i) -> p s bb j i", s=8, j=4, bb=2)
                    if mat == 1:
                        dst = dst_sup[:].rearrange(
                            "p (s fb bb gi i) -> p s fb bb gi i", s=8, fb=NFB, bb=2, gi=2)[
                            :, :, grp // 2, :, grp % 2, :].rearrange(
                            "p s bb (j i) -> p s bb j i", j=4)
                    else:
                        dst = dst_sup[:].rearrange(
                            "p (s bb g i) -> p s bb g i", s=8, bb=2, g=16)[
                            :, :, :, grp, :].rearrange(
                            "p s bb (j i) -> p s bb j i", j=4)
                    e.tensor_copy(dst, srcv).then_inc(p.sem["v"], 1)
                p.on("vector", fn)
                st[("mwT_ev_v",)] = p.bump("v")
        mwT_done = st[("mwT_ev_v",)]
        p.wait("tensor", "v", mwT_done)

        if S > 0:
            p.wait("gpsimd", "sa0", p.now("sa0"))
            p.wait("gpsimd", "sa1", p.now("sa1"))
        for b in range(2):
            for th in range(TH):
                for fb in range(NFB):
                    for tb in range(THW // 512):
                        bank = PSB[tb % 2]
                        prev = st.get(("u_ev_s", tb % 2))
                        if prev:
                            p.wait("tensor", "s", prev)

                        def fn(e, b=b, th=th, fb=fb, tb=tb, bank=bank):
                            for hs in range(8):
                                c0 = hs * (NFB * 256) + fb * 256 + b * 128
                                ap = A["mw1Ts"][:, c0:c0 + 128]
                                mm = e.matmul(bank[:], ap,
                                              A["ln2fm"][:].rearrange("p (j b2 t) -> p j b2 t", j=8, b2=2)[
                                                  :, hs, b, th * THW + 512 * tb:th * THW + 512 * tb + 512],
                                              start=(hs == 0), stop=(hs == 7))
                            mm.then_inc(p.sem["t"], 1)
                        p.on("tensor", fn)
                        mm1_t = p.bump("t")
                        p.wait("scalar", "t", mm1_t)

                        def fn(e, b=b, fb=fb, tb=tb, bank=bank, S=S):
                            col = 2 * (S * NFB + fb) + b
                            e.activation(A["usup"][:, fb * THW + 512 * tb:fb * THW + 512 * tb + 512], bank[:],
                                         AF.Gelu_apprx_tanh, bias=A["mb1T"][:, col:col + 1],
                                         scale=1.0).then_inc(p.sem["s"], 1)
                        p.on("scalar", fn)
                        st[("u_ev_s", tb % 2)] = p.bump("s")
                u_done = p.now("s")
                p.wait("tensor", "s", u_done)
                for t8 in range(THW // 128):
                    prev = st.get(("ffev_v",))
                    if prev:
                        p.wait("tensor", "v", prev)

                    def fn(e, b=b, th=th, t8=t8):
                        for fb in range(NFB):
                            lhs = A["usup"][:, fb * THW + 128 * t8:fb * THW + 128 * t8 + 128]
                            c0 = fb * 2048 + b * 1024
                            e.matmul(PSB[4][:], lhs, A["mw2Ts"][:, c0:c0 + 512],
                                     start=(fb == 0), stop=(fb == NFB - 1))
                            mm = e.matmul(PSB[5][:], lhs, A["mw2Ts"][:, c0 + 512:c0 + 1024],
                                          start=(fb == 0), stop=(fb == NFB - 1))
                        mm.then_inc(p.sem["t"], 1)
                    p.on("tensor", fn)
                    mm2_t = p.bump("t")
                    p.wait("vector", "t", mm2_t)
                    eslot = t8 % 2
                    preva = st.get(("acc_dma", t8 % 2))
                    if preva:
                        p.wait("vector", f"sa{t8 % 2}", preva)

                    def fn(e, eslot=eslot):
                        e.tensor_copy(A["ffev"][:, C * eslot:C * eslot + 512], PSB[4][:])
                        e.tensor_copy(A["ffev"][:, C * eslot + 512:C * eslot + 1024], PSB[5][:]).then_inc(p.sem["v"], 1)
                    p.on("vector", fn)
                    fe_v = p.bump("v")
                    st[("ffev_v",)] = fe_v
                    p.wait("gpsimd", "v", fe_v)
                    sem = p.ensure(f"sa{t8 % 2}")
                    row0 = th * THW + 128 * t8

                    def fn(e, b=b, row0=row0, eslot=eslot, S=S, sem=sem):
                        if S == 0:
                            e.dma_start(A["ffnacc"][b, row0:row0 + 128, :],
                                        A["ffev"][:, C * eslot:C * eslot + C]).then_inc(sem, 16)
                        else:
                            e.dma_start(A["ffnacc"][b, row0:row0 + 128, :],
                                        A["ffev"][:, C * eslot:C * eslot + C],
                                        accum_op=ALU.add).then_inc(sem, 16)
                    p.on("gpsimd", fn)
                    st[("acc_dma", t8 % 2)] = p.bump(f"sa{t8 % 2}", 16)
    p.barrier()
    if upto in ("ffn", "sup1"):
        return _finish()
    sbuf_free("ffev", "usup", "mw2Ts", "mw1Ts", "mwfm", "wrep", "twld",
              "ln2fm")

    sbuf_scoped("f_rd", [128, 2 * C], F32)
    sbuf_scoped("f_xp", [128, 2 * C], F32)
    sbuf_scoped("f_ff", [128, 2 * C], F32)
    sbuf_scoped("f_out", [128, 2 * C], F32)
    _fin_v = []
    _fin_s = []
    for b in range(2):
        for tt in range(TT):
            slot = tt % 2
            if len(_fin_v) >= 2:
                p.wait("sync", "v", _fin_v[-2])
            rv = dma("sync", A["f_rd"][:, C * slot:C * slot + C],
                     A["ffnacc"][b, 128 * tt:128 * tt + 128, :], f"sfr{slot}")
            xv = dma("sync", A["f_xp"][:, C * slot:C * slot + C],
                     A["xprime"][b, 128 * tt:128 * tt + 128, :], f"sfx{slot}")
            p.wait("vector", f"sfr{slot}", rv)
            p.wait("vector", f"sfx{slot}", xv)
            if len(_fin_s) >= 2:
                p.wait("vector", "s", _fin_s[-2])

            def fn(e, b=b, slot=slot):
                e.tensor_add(A["f_ff"][:, C * slot:C * slot + C],
                             A["f_rd"][:, C * slot:C * slot + C], A["mb2bc"][:, C * b:C * b + C]).then_inc(p.sem["v"], 1)
            p.on("vector", fn)
            ff_v = p.bump("v")
            ln_stats(A["f_ff"][:, C * slot:C * slot + C])
            p.wait("scalar", f"sfx{slot}", xv)

            def fn(e, slot=slot):
                e.activation(A["f_xp"][:, C * slot:C * slot + C], A["f_xp"][:, C * slot:C * slot + C],
                             AF.Identity, bias=A["lnt"][:, 12:13], scale=1.0).then_inc(p.sem["s"], 1)
            p.on("scalar", fn)
            fs = p.bump("s")
            _fin_s.append(fs)
            p.wait("vector", "s", fs)
            if len(_fin_out := st.setdefault("_fin_out", [])) >= 2:
                p.wait("vector", f"so{slot}", _fin_out[-2])

            def fn(e, slot=slot):
                e.scalar_tensor_tensor(A["f_out"][:, C * slot:C * slot + C],
                                       A["f_ff"][:, C * slot:C * slot + C],
                                       A["lnt"][:, 15:16], A["f_xp"][:, C * slot:C * slot + C],
                                       op0=ALU.mult, op1=ALU.add).then_inc(p.sem["v"], 1)
            p.on("vector", fn)
            ov = p.bump("v")
            _fin_v.append(ov)
            p.wait("sync", "v", ov)
            st["_fin_out"].append(dma("sync", A["out"][b, 128 * tt:128 * tt + 128, :],
                                      A["f_out"][:, C * slot:C * slot + C], f"so{slot}"))
    p.barrier()

    p.emit()
    nc.finalize()
    return nc


_NC_CACHE = {}


def get_nc(T):
    if T not in _NC_CACHE:
        _NC_CACHE[T] = build_nc(T)
    return _NC_CACHE[T]


def kernel(**inputs):
    T = inputs["L0_attn_x"].shape[1]
    nc = get_nc(T)
    consts = make_consts(T)
    x = np.ascontiguousarray(np.asarray(inputs["L0_attn_x"], dtype=np.float32))
    nb = x.shape[0]
    ncores = nb // BC
    weights = {}
    name_map = {
        "c_attn_w": "L0_attn_c_attn_w", "c_attn_b": "L0_attn_c_attn_b",
        "q2l_w": "L0_attn_q2l_w", "q2l_b": "L0_attn_q2l_b",
        "kv2l_w": "L0_attn_kv2l_w", "kv2l_b": "L0_attn_kv2l_b",
        "l2o_w": "L0_attn_l2o_w", "l2o_b": "L0_attn_l2o_b",
        "wqk_w": "L0_attn_wqk_w", "wqk_b": "L0_attn_wqk_b",
        "cproj_w": "L0_attn_cproj_w", "cproj_b": "L0_attn_cproj_b",
        "gate_w1": "L0_ffn_gate_w1", "gate_b1": "L0_ffn_gate_b1",
        "gate_w2": "L0_ffn_gate_w2", "gate_b2": "L0_ffn_gate_b2",
        "W1": "L0_ffn_W1", "b1": "L0_ffn_b1",
        "W2": "L0_ffn_W2", "b2": "L0_ffn_b2",
        "TW1": "L0_ffn_TW1", "Tb1": "L0_ffn_Tb1",
        "TW2": "L0_ffn_TW2", "Tb2": "L0_ffn_Tb2",
    }
    for k, src in name_map.items():
        weights[k] = np.ascontiguousarray(np.asarray(inputs[src], dtype=np.float32))
    in_maps = []
    for c in range(ncores):
        m = {"x": x[BC * c:BC * c + BC]}
        m.update(weights)
        m.update(consts)
        in_maps.append(m)
    res = run_bass_kernel_spmd(nc, in_maps, core_ids=list(range(ncores)))
    out = np.concatenate([np.asarray(r["out"], dtype=np.float32) for r in res.results], axis=0)
    return out


# version 10
# speedup vs baseline: 1.1487x; 1.1487x over previous
import numpy as np
import ml_dtypes

import concourse.bass as bass
import concourse.mybir as mybir
from concourse.bass_utils import run_bass_kernel_spmd

F32 = mybir.dt.float32
BF16 = mybir.dt.bfloat16
AF = mybir.ActivationFunctionType
ALU = mybir.AluOpType
AX = mybir.AxisListType
BFNP = ml_dtypes.bfloat16

NB = 16
NCORES = 8
BC = 2
C = 1024
NH = 16
HD = 64
LAT = 32
FF = 4096
NE = 8
EPS = 1e-5
ISQ = 32 ** -0.5
FSUP = 1024


def _rope_tables(T):
    try:
        import jax
        import jax.numpy as jnp
        cpu = jax.devices("cpu")[0]
        with jax.default_device(cpu):
            inv_freq = 1.0 / 10000.0 ** (jnp.arange(0, LAT, 2, dtype=jnp.float32) / LAT)
            t = jnp.arange(T, dtype=jnp.float32)
            freqs = jnp.outer(t, inv_freq)
            cos = jnp.repeat(jnp.cos(freqs), 2, axis=-1)[:, :LAT]
            sin = jnp.repeat(jnp.sin(freqs), 2, axis=-1)[:, :LAT]
            return np.asarray(cos, np.float32), np.asarray(sin, np.float32)
    except Exception:
        a = np.arange(0, LAT, 2, dtype=np.float32)
        inv_freq = (np.float32(1.0) / np.float32(10000.0) ** (a / np.float32(LAT))).astype(np.float32)
        t = np.arange(T, dtype=np.float32)
        freqs = np.outer(t, inv_freq).astype(np.float32)
        cos = np.repeat(np.cos(freqs).astype(np.float32), 2, axis=-1)[:, :LAT]
        sin = np.repeat(np.sin(freqs).astype(np.float32), 2, axis=-1)[:, :LAT]
        return cos, sin


def make_consts(T):
    cos, sin = _rope_tables(T)
    cos_pack = np.tile(cos.T, (4, 1)).astype(np.float32)
    sgn = np.where(np.arange(LAT) % 2 == 0, -1.0, 1.0).astype(np.float32)
    sin_pack = np.tile((sin * sgn[None, :]).T, (4, 1)).astype(np.float32)
    ident_bf = np.eye(128, dtype=BFNP)
    diagmask = np.triu(np.ones((128, 128), np.float32)).astype(BFNP)
    headones = np.zeros((128, 4), BFNP)
    for hp in range(4):
        headones[32 * hp:32 * hp + 32, hp] = 1
    ones4x128 = np.ones((4, 128), BFNP)
    mask16 = np.zeros((4, 16), np.float32)
    for pk in range(4):
        for j in range(4):
            mask16[j, 4 * pk + j] = 1.0
    sel4 = np.zeros((128, 128), np.float32)
    sel4[32, 0:32] = 1.0
    sel4[64, 32:64] = 1.0
    sel4[96, 64:96] = 1.0
    sel4[0, 96:128] = 1.0
    m01 = np.zeros((128, 32), BFNP)
    for e in range(NE):
        for b in range(2):
            for i in range(16):
                m01[16 * e + i, 16 * b + i] = 1
    p_e = np.zeros((8, 128), BFNP)
    for e in range(NE):
        p_e[e, 16 * e:16 * e + 16] = 1
    sel2 = np.zeros((2, 256), np.float32)
    sel2[0, 0:128] = 1.0
    sel2[1, 128:256] = 1.0
    ones1x128 = np.ones((1, 128), np.float32)
    ident_f32 = np.eye(128, dtype=np.float32)
    return {
        "cos_pack": cos_pack, "sin_pack": sin_pack, "ident_bf": ident_bf,
        "diagmask": diagmask, "headones": headones, "ones4x128": ones4x128,
        "mask16": mask16, "sel4": sel4,
        "m01": m01, "p_e": p_e, "sel2": sel2, "ones1x128": ones1x128,
        "ident_f32": ident_f32,
    }


CONST_SPECS = [
    ("cos_pack", None, F32), ("sin_pack", None, F32),
    ("ident_bf", [128, 128], BF16), ("diagmask", [128, 128], BF16),
    ("headones", [128, 4], BF16), ("ones4x128", [4, 128], BF16),
    ("mask16", [4, 16], F32), ("sel4", [128, 128], F32),
    ("m01", [128, 32], BF16),
    ("p_e", [8, 128], BF16), ("sel2", [2, 256], F32),
    ("ones1x128", [1, 128], F32), ("ident_f32", [128, 128], F32),
]


class Prog:
    ENGS = ("sync", "scalar", "vector", "tensor", "gpsimd")

    def __init__(self, nc):
        self.nc = nc
        self.ops = {e: [] for e in self.ENGS}
        self.sem = {}
        self.cnt = {}
        for name in ("t", "v", "s", "g"):
            self.ensure(name)

    def ensure(self, name):
        if name not in self.sem:
            self.sem[name] = self.nc.alloc_semaphore("sem_" + name)
            self.cnt[name] = 0
        return self.sem[name]

    def on(self, eng, fn):
        self.ops[eng].append(fn)

    def bump(self, name, k=1):
        self.ensure(name)
        self.cnt[name] += k
        return self.cnt[name]

    def now(self, name):
        return self.cnt[name]

    def wait(self, eng, name, val):
        if val and val > 0:
            sem = self.ensure(name)
            self.ops[eng].append(lambda e: e.wait_ge(sem, val))

    def snap(self):
        return dict(self.cnt)

    def barrier(self, snap=None):
        snap = snap or self.snap()
        for eng in self.ENGS:
            for name, val in snap.items():
                self.wait(eng, name, val)

    def emit(self):
        with self.nc.Block() as blk:
            for eng in self.ENGS:
                ops = self.ops[eng]

                def body(e, ops=ops):
                    for fn in ops:
                        fn(e)
                getattr(blk, eng)(body)


def build_nc(T, upto=None):
    TT = T // 128
    NTB = T // 512
    NKT = T // 128
    NSUP = FF // FSUP
    G64 = FSUP // 64
    NFB = FSUP // 128
    TH = max(1, T // 512)
    THW = T // TH

    nc = bass.Bass("TRN2", target_bir_lowering=False)
    p = Prog(nc)
    A = {}
    guards = {}

    def dram(name, shape, dt=F32, kind="ExternalInput"):
        A[name] = nc.dram_tensor(name, shape, dt, kind=kind)

    def sbuf(name, shape, dt):
        A[name] = nc.alloc_sbuf_tensor(name, shape, dt)

    def sbuf_scoped(name, shape, dt):
        g = nc.sbuf_tensor(name, shape, dt)
        A[name] = g.__enter__()
        guards[name] = g

    def sbuf_free(*names):
        for n in names:
            guards[n].__exit__(None, None, None)
            del guards[n]
        p.barrier()

    dram("x", [BC, T, C])
    dram("c_attn_w", [3 * C, C]); dram("c_attn_b", [3 * C])
    dram("q2l_w", [LAT, HD]); dram("q2l_b", [LAT])
    dram("kv2l_w", [LAT, HD]); dram("kv2l_b", [LAT])
    dram("l2o_w", [HD, LAT]); dram("l2o_b", [HD])
    dram("wqk_w", [LAT, LAT]); dram("wqk_b", [LAT])
    dram("cproj_w", [C, C]); dram("cproj_b", [C])
    dram("gate_w1", [C, C]); dram("gate_b1", [C])
    dram("gate_w2", [NE, C]); dram("gate_b2", [NE])
    dram("W1", [FF, C]); dram("b1", [FF])
    dram("W2", [C, FF]); dram("b2", [C])
    dram("TW1", [NE, FF, C]); dram("Tb1", [NE, FF])
    dram("TW2", [NE, C, FF]); dram("Tb2", [NE, C])
    for cname, shp, cdt in CONST_SPECS:
        dram(cname, shp or [128, T], cdt)
    dram("out", [BC, T, C], F32, kind="ExternalOutput")
    dram("xprime", [BC, T, C], F32, kind="Internal")
    dram("ln2scr", [BC, T, C], BF16, kind="Internal")
    dram("ffnacc", [BC, T, C], F32, kind="Internal")

    PSB = [nc.alloc_psum_tensor(f"pb{i}", [128, 512], F32) for i in range(6)]
    psg = {}

    _psn = [0]

    def psum_scoped(name, shape, dt):
        _psn[0] += 1
        g = nc.psum_tensor(f"{name}_{_psn[0]}", shape, dt)
        h = g.__enter__()
        psg[name] = g
        return h

    def psum_free(*names):
        for n in names:
            psg[n].__exit__(None, None, None)
            del psg[n]
        p.barrier()

    psbf_a = psum_scoped("psbf_a", [128, 1024], BF16)
    psbf_b = psum_scoped("psbf_b", [128, 1024], BF16)
    PSY = [PSB[4], PSB[5], None, None]

    for cname, shp, cdt in CONST_SPECS:
        if cname in ("cos_pack", "sin_pack"):
            continue
        sbuf("k_" + cname, shp or [128, T], cdt)
    sbuf("q2l_st", [128, 64], BF16)
    sbuf("kv2l_st", [128, 64], BF16)
    sbuf("wqk_st", [128, 128], BF16)
    sbuf("l2o_st", [128, 64], BF16)
    sbuf("kv2l_wT", [128, 32], BF16)
    sbuf("gw2T", [128, 64], BF16)
    sbuf("cab", [128, 24], F32)
    sbuf("cab_r", [24, 128], F32)
    sbuf("gb1c", [128, 8], F32)
    sbuf("gb1_r", [8, 128], F32)
    sbuf("b1_r", [32, 128], F32)
    sbuf("q2l_bb", [128, 1], F32)
    sbuf("kv2l_bb", [128, 1], F32)
    sbuf("wqk_bb", [128, 1], F32)
    sbuf("l2o_bb", [128, 1], F32)
    sbuf("kvb_row", [1, 32], F32)
    sbuf("kvb_bc", [128, 32], F32)
    sbuf("cpb_row", [1, C], F32)
    sbuf("cpb_bc", [128, C], F32)
    sbuf("gb2c", [8, 1], F32)
    sbuf("b1T", [128, FF // 128], F32)
    sbuf("mb1T", [128, (FF // 128) * 2], F32)
    sbuf("mb2row", [2, C], F32)
    sbuf("mb2bc", [128, 2 * C], F32)
    sbuf("g32", [8, 4], F32)
    sbuf("gbf", [8, 2], BF16)
    sbuf("gcols", [8, 32], BF16)
    sbuf("Gpat", [128, 32], BF16)
    sbuf("negcm", [4, 16], BF16)
    sbuf("negc_bc", [128, 16], F32)
    sbuf("qn2", [4, 4], F32)
    sbuf("kn2", [4, 4], F32)
    sbuf("negc", [4, 4], F32)
    sbuf("lnt", [128, 64], F32)
    sbuf("chn", [4, 8], F32)
    sbuf("g_parts", [8, 8], F32)
    sbuf("g_scr", [8, 512], BF16)
    sbuf("sw_scr", [128, 192], BF16)
    sbuf("gw2_raw", [8, C], BF16)
    sbuf("ceps", [128, 1], F32)
    sbuf_scoped("k_cos_pack", [128, T], F32)
    sbuf_scoped("k_sin_pack", [128, T], F32)

    def dma(eng, dst, src, inc, slow=False):
        sem = p.ensure(inc)
        p.ensure(inc)
        if slow:
            p.on(eng, lambda e: e.dma_start(dst, src, allow_slow_non_contiguous=True).then_inc(sem, 16))
        else:
            p.on(eng, lambda e: e.dma_start(dst, src).then_inc(sem, 16))
        return p.bump(inc, 16)

    for cname, shp, cdt in CONST_SPECS:
        dma("sync", A["k_" + cname][:], A[cname][:], "sd0")
    dma("sync", A["cab_r"][:], A["c_attn_b"][:].rearrange("(a p) -> a p", p=128), "sd0")
    dma("sync", A["gb1_r"][:], A["gate_b1"][:].rearrange("(a p) -> a p", p=128), "sd0")
    for hp in range(4):
        dma("sync", A["q2l_bb"][32 * hp:32 * hp + 32, :], A["q2l_b"][:].rearrange("(l a) -> l a", a=1), "sd0", slow=True)
        dma("sync", A["kv2l_bb"][32 * hp:32 * hp + 32, :], A["kv2l_b"][:].rearrange("(l a) -> l a", a=1), "sd0", slow=True)
        dma("sync", A["wqk_bb"][32 * hp:32 * hp + 32, :], A["wqk_b"][:].rearrange("(l a) -> l a", a=1), "sd0", slow=True)
    for hh in range(2):
        dma("sync", A["l2o_bb"][64 * hh:64 * hh + 64, :], A["l2o_b"][:].rearrange("(l a) -> l a", a=1), "sd0", slow=True)
    dma("sync", A["gb2c"][:], A["gate_b2"][:].rearrange("(e a) -> e a", a=1), "sd0", slow=True)
    dma("sync", A["b1_r"][:], A["b1"][:].rearrange("(a p) -> a p", p=128), "sd0")
    dma("sync", A["kvb_row"][:], A["kv2l_b"][:].rearrange("(a l) -> a l", a=1), "sd0")
    dma("sync", A["cpb_row"][:], A["cproj_b"][:].rearrange("(a c) -> a c", a=1), "sd0")
    dma("gpsimd", A["sw_scr"][0:32, 0:64], A["q2l_w"][:], "sg0")
    dma("gpsimd", A["sw_scr"][0:32, 64:128], A["kv2l_w"][:], "sg0")
    dma("gpsimd", A["sw_scr"][0:64, 128:160], A["l2o_w"][:], "sg0")
    dma("gpsimd", A["sw_scr"][0:32, 160:192], A["wqk_w"][:], "sg0")
    dma("gpsimd", A["gw2_raw"][:], A["gate_w2"][:], "sg0")
    p0_sd = p.now("sd0")
    p0_sg = p.now("sg0")

    def fn(e):
        e.memset(A["q2l_st"][:], 0)
        e.memset(A["kv2l_st"][:], 0)
        e.memset(A["ceps"][:], EPS)
        e.memset(A["wqk_st"][:], 0).then_inc(p.sem["v"], 1)
    p.on("vector", fn)
    memset_v = p.bump("v")

    p.wait("tensor", "sg0", p0_sg)
    p.wait("tensor", "sd0", p0_sd)

    def fn(e):
        idt = A["k_ident_bf"]
        e.transpose(psbf_a[0:64, 0:32], A["sw_scr"][0:32, 0:64], idt[0:32, 0:32])
        e.transpose(psbf_a[64:128, 32:64], A["sw_scr"][0:32, 0:64], idt[0:32, 0:32], tile_position=(0, 64))
        e.transpose(psbf_a[0:64, 64:96], A["sw_scr"][0:32, 64:128], idt[0:32, 0:32])
        e.transpose(psbf_a[64:128, 96:128], A["sw_scr"][0:32, 64:128], idt[0:32, 0:32], tile_position=(0, 64))
        e.transpose(psbf_a[0:64, 128:160], A["sw_scr"][0:32, 64:128], idt[0:32, 0:32])
        e.transpose(psbf_a[64:128, 160:192], A["sw_scr"][0:32, 64:128], idt[0:32, 0:32], tile_position=(0, 64))
        for gq in range(4):
            e.transpose(psbf_a[32 * gq:32 * gq + 32, 192 + 64 * gq:256 + 64 * gq],
                        A["sw_scr"][0:64, 128:160], idt[0:64, 0:64], tile_position=(0, 32 * gq))
        for gq in range(4):
            e.transpose(psbf_a[32 * gq:32 * gq + 32, 448 + 32 * gq:480 + 32 * gq],
                        A["sw_scr"][0:32, 160:192], idt[0:32, 0:32], tile_position=(0, 32 * gq))
        for j in range(8):
            mm = e.transpose(psbf_b[0:128, 8 * j:8 * j + 8],
                             A["gw2_raw"][0:8, 128 * j:128 * j + 128], idt[0:8, 0:8])
        idf = A["k_ident_f32"]
        e.transpose(PSB[3][0:128, 0:24], A["cab_r"][0:24, :], idf[0:24, 0:24])
        e.transpose(PSB[3][0:128, 24:32], A["gb1_r"][0:8, :], idf[0:8, 0:8])
        mm = e.transpose(PSB[3][0:128, 32:64], A["b1_r"][0:32, :], idf[0:32, 0:32])
        mm.then_inc(p.sem["t"], 1)
    p.on("tensor", fn)
    smallT = p.bump("t")

    p.wait("scalar", "t", smallT)
    p.wait("scalar", "v", memset_v)

    def fn(e):
        e.copy(A["q2l_st"][0:64, 0:32], psbf_a[0:64, 0:32])
        e.copy(A["q2l_st"][64:128, 32:64], psbf_a[64:128, 32:64])
        e.copy(A["kv2l_st"][0:64, 0:32], psbf_a[0:64, 64:96])
        e.copy(A["kv2l_st"][64:128, 32:64], psbf_a[64:128, 96:128])
        e.copy(A["kv2l_wT"][0:64, :], psbf_a[0:64, 128:160])
        e.copy(A["kv2l_wT"][64:128, :], psbf_a[64:128, 160:192])
        for gq in range(4):
            e.copy(A["l2o_st"][32 * gq:32 * gq + 32, :], psbf_a[32 * gq:32 * gq + 32, 192 + 64 * gq:256 + 64 * gq])
            e.copy(A["wqk_st"][32 * gq:32 * gq + 32, 32 * gq:32 * gq + 32],
                   psbf_a[32 * gq:32 * gq + 32, 448 + 32 * gq:480 + 32 * gq])
        e.copy(A["gw2T"][:], psbf_b[0:128, 0:64])
        e.copy(A["cab"][:], PSB[3][0:128, 0:24])
        e.copy(A["gb1c"][:], PSB[3][0:128, 24:32])
        e.copy(A["b1T"][:], PSB[3][0:128, 32:64]).then_inc(p.sem["s"], 1)
    p.on("scalar", fn)
    smallT_s = p.bump("s")

    p.wait("tensor", "s", smallT_s)

    def fn(e):
        e.matmul(PSB[0][0:128, 0:32], A["k_ones1x128"][:], A["kvb_row"][:], start=True, stop=True)
        e.matmul(PSB[1][:], A["k_ones1x128"][:], A["cpb_row"][0:1, 0:512], start=True, stop=True)
        e.matmul(PSB[2][:], A["k_ones1x128"][:], A["cpb_row"][0:1, 512:1024], start=True, stop=True).then_inc(p.sem["t"], 1)
    p.on("tensor", fn)
    bcb_t = p.bump("t")
    p.wait("scalar", "t", bcb_t)

    def fn(e):
        e.copy(A["kvb_bc"][:], PSB[0][0:128, 0:32])
        e.copy(A["cpb_bc"][:, 0:512], PSB[1][:])
        e.copy(A["cpb_bc"][:, 512:1024], PSB[2][:]).then_inc(p.sem["s"], 1)
    p.on("scalar", fn)
    p.bump("s")

    st = {}

    class _Stop(Exception):
        pass

    def _ck(tag):
        if upto == tag:
            raise _Stop()

    def _finish():
        p.barrier()
        p.emit()
        nc.finalize()
        return nc

    def ln_stats(src_ap):
        slot = st.get("lnt_rr", 0) % 4
        st["lnt_rr"] = st.get("lnt_rr", 0) + 1
        o = 16 * slot
        lt = A["lnt"]
        prev = st.get(("lnt_free", slot))
        if prev:
            p.wait("vector", "s", prev)

        def fn(e, src_ap=src_ap, o=o):
            e.drain()
            e.bn_stats(lt[:, o:o + 6], src_ap[:, 0:512])
            e.bn_stats(lt[:, o + 6:o + 12], src_ap[:, 512:1024])
            e.drain()
            e.bn_aggr(lt[:, o + 12:o + 14], lt[:, o:o + 12]).then_inc(p.sem["v"], 1)
        p.on("vector", fn)
        vv = p.bump("v")
        p.wait("scalar", "v", vv)

        def fn(e, o=o):
            e.activation(lt[:, o + 14:o + 15], lt[:, o + 13:o + 14], AF.Sqrt,
                         bias=A["ceps"][:], scale=1.0).then_inc(p.sem["s"], 1)
        p.on("scalar", fn)
        sv = p.bump("s")
        p.wait("vector", "s", sv)

        def fn(e, o=o):
            e.reciprocal(lt[:, o + 15:o + 16], lt[:, o + 14:o + 15])
            e.drain()
            e.tensor_scalar(lt[:, o + 12:o + 13], lt[:, o + 12:o + 13], lt[:, o + 15:o + 16], -1.0,
                            op0=ALU.mult, op1=ALU.mult).then_inc(p.sem["v"], 1)
        p.on("vector", fn)
        vv2 = p.bump("v")
        p.wait("scalar", "v", vv2)
        return slot, lt[:, o + 12:o + 13], lt[:, o + 15:o + 16]

    def ln_done(slot):
        st[("lnt_free", slot)] = p.now("s")

    def attention(b):
        sbuf_scoped(f"yfm{b}", [128, 8 * T], BF16)
        sbuf_scoped(f"qf{b}", [128, 4 * T], BF16)
        sbuf_scoped(f"kl{b}", [128, 4 * T], BF16)
        sbuf_scoped(f"vaug{b}", [128, NH * NKT * 33], BF16)
        sbuf_scoped(f"ln1fm{b}", [128, 8 * T], BF16)
        sbuf_scoped(f"xt{b}", [128, 2 * C], F32)
        sbuf_scoped(f"ln1tm{b}", [128, 2 * C], BF16)
        lnfm = A[f"ln1fm{b}"]
        for tt in range(TT):
            slot = tt % 2
            xt = A[f"xt{b}"][:, C * slot:C * slot + C]
            if tt >= 2:
                p.wait("sync", "s", st[("ln1_use", b, tt - 2)])
            xv = dma("sync", xt, A["x"][b, 128 * tt:128 * tt + 128, :], f"sx{slot}")
            p.wait("vector", f"sx{slot}", xv)
            lslot, negms, rstd = ln_stats(xt)

            def fn(e, slot=slot, xt=xt, negms=negms, rstd=rstd):
                lt = A[f"ln1tm{b}"][:, C * slot:C * slot + C]
                e.activation(lt, xt, AF.Identity, bias=negms,
                             scale=rstd).then_inc(p.sem["s"], 1)
            p.on("scalar", fn)
            lns = p.bump("s")
            ln_done(lslot)
            st[("ln1_use", b, tt)] = lns
            p.wait("tensor", "s", lns)
            if tt >= 1:
                p.wait("tensor", "v", st[("ln1fm_v", b, tt - 1)])

            def fn(e, slot=slot):
                lt = A[f"ln1tm{b}"][:, C * slot:C * slot + C]
                for j in range(8):
                    mm = e.transpose(psbf_a[:, 128 * j:128 * j + 128], lt[:, 128 * j:128 * j + 128], A["k_ident_bf"][:])
                mm.then_inc(p.sem["t"], 1)
            p.on("tensor", fn)
            trt = p.bump("t")
            st[("ln1tr", b, tt)] = trt
            p.wait("vector", "t", trt)

            def fn(e, tt=tt):
                dst = lnfm[:].rearrange("p (j t) -> p j t", j=8)[:, :, 128 * tt:128 * tt + 128]
                e.tensor_copy(dst, psbf_a[:].rearrange("p (j t) -> p j t", j=8)).then_inc(p.sem["v"], 1)
            p.on("vector", fn)
            st[("ln1fm_v", b, tt)] = p.bump("v")
        ln1_v = st[("ln1fm_v", b, TT - 1)]
        sbuf_free(f"ln1tm{b}", f"xt{b}")
        _ck(f"ln1_{b}")

        sbuf_scoped(f"cawld{b}", [128, 2 * C], BF16)
        sbuf_scoped(f"cawst{b}", [128, 2 * 1024], BF16)
        sbuf_scoped(f"qkv{b}", [128, 2 * T], BF16)
        sbuf_scoped(f"latraw{b}", [128, T], F32)
        sbuf_scoped(f"rsc{b}", [128, 3 * 512], F32)
        sbuf_scoped(f"qrope{b}", [128, T], BF16)
        qkv_t = A[f"qkv{b}"]

        def fn(e):
            va = A[f"vaug{b}"][:].rearrange("p (k c) -> p k c", c=33)
            e.memset(va[:, :, 32:33], 1.0).then_inc(p.sem["v"], 1)
        p.on("vector", fn)
        vones_v = p.bump("v")

        st[("qkv_seq", b)] = 0

        def qkv_chunk(co):
            seq = st[("qkv_seq", b)]
            st[("qkv_seq", b)] = seq + 1
            ld = seq % 2
            ss = seq % 2
            ring = seq % 2
            prev = st.get(("cawT_seq", b, seq - 2))
            if prev:
                p.wait("gpsimd", "t", prev)
            cav = dma("gpsimd", A[f"cawld{b}"][:, C * ld:C * ld + C],
                      A["c_attn_w"][128 * co:128 * co + 128, :], f"scw{ld}")
            p.wait("tensor", f"scw{ld}", cav)
            if seq == 0:
                p.wait("tensor", "v", ln1_v)
            prev = st.get(("cawst_v_seq", b, seq - 1))
            if prev:
                p.wait("tensor", "v", prev)

            def fn(e, ld=ld):
                src = A[f"cawld{b}"][:, C * ld:C * ld + C]
                for j in range(8):
                    mm = e.transpose(psbf_a[:, 128 * j:128 * j + 128], src[:, 128 * j:128 * j + 128], A["k_ident_bf"][:])
                mm.then_inc(p.sem["t"], 1)
            p.on("tensor", fn)
            cawT = p.bump("t")
            st[("cawT_seq", b, seq)] = cawT
            p.wait("vector", "t", cawT)
            prev = st.get(("caw_mm_t_seq", b, seq - 2))
            if prev:
                p.wait("vector", "t", prev)

            def fn(e, ss=ss):
                e.tensor_copy(A[f"cawst{b}"][:, 1024 * ss:1024 * ss + 1024], psbf_a[:, 0:1024]).then_inc(p.sem["v"], 1)
            p.on("vector", fn)
            cawst_v = p.bump("v")
            st[("cawst_v_seq", b, seq)] = cawst_v

            p.wait("tensor", "v", cawst_v)
            for tb in range(NTB):
                bank = PSB[4 + tb % 2]
                prev = st.get(("qkvbank_v", b, tb % 2))
                if prev:
                    p.wait("tensor", "v", prev)
                prev = st.get(("qkv_cons_t", b, ring))
                if prev:
                    p.wait("vector", "t", prev)

                def fn(e, ss=ss, tb=tb, bank=bank):
                    for j in range(8):
                        mm = e.matmul(bank[:],
                                      A[f"cawst{b}"][:, 1024 * ss + 128 * j:1024 * ss + 128 * j + 128],
                                      lnfm[:].rearrange("p (jj t) -> p jj t", jj=8)[:, j, 512 * tb:512 * tb + 512],
                                      start=(j == 0), stop=(j == 7))
                    mm.then_inc(p.sem["t"], 1)
                p.on("tensor", fn)
                mmt = p.bump("t")
                p.wait("vector", "t", mmt)

                def fn(e, ring=ring, tb=tb, co=co, bank=bank):
                    dst = qkv_t[:, T * ring + 512 * tb:T * ring + 512 * tb + 512]
                    e.tensor_scalar(dst, bank[:], A["cab"][:, co:co + 1], None,
                                    op0=ALU.add, op1=ALU.bypass).then_inc(p.sem["v"], 1)
                p.on("vector", fn)
                st[("qkvbank_v", b, tb % 2)] = p.bump("v")
            st[("qkv_ev", b, co)] = p.now("v")
            st[("caw_mm_t_seq", b, seq)] = p.now("t")
            return ring

        def qkv_consumed(*rings):
            for r in rings:
                st[("qkv_cons_t", b, r)] = p.now("t")

        def rope_pack(src, dst):
            sh = A[f"rsc{b}"][:, 0:512]
            t1 = A[f"rsc{b}"][:, 512:1024]
            t2 = A[f"rsc{b}"][:, 1024:1536]
            mask = []
            for i in range(16):
                mask += [2 * i + 1, 2 * i]

            def fn(e):
                for blk in range(NTB):
                    sl = slice(512 * blk, 512 * blk + 512)
                    e.drain()
                    e.stream_shuffle(sh, src[:, sl], mask)
                    e.tensor_mul(t1, src[:, sl], A["k_cos_pack"][:, sl])
                    e.drain()
                    e.tensor_mul(t2, sh, A["k_sin_pack"][:, sl])
                    e.drain()
                    ta = e.tensor_add(dst[:, sl], t1, t2)
                ta.then_inc(p.sem["v"], 1)
            p.on("vector", fn)
            return p.bump("v")

        def latent_mms(r0, r1, stn, bias_name, out_pref):
            for tb in range(NTB):
                bank = PSB[tb % 2]
                prev = st.get(("latbank_s", b, tb % 2))
                if prev:
                    p.wait("tensor", "s", prev)

                def fn(e, r0=r0, r1=r1, tb=tb, bank=bank, stn=stn):
                    e.matmul(bank[0:64, :], A[stn][:, 0:64],
                             qkv_t[:, T * r0 + 512 * tb:T * r0 + 512 * tb + 512],
                             start=True, stop=True)
                    e.matmul(bank[64:128, :], A[stn][:, 0:64],
                             qkv_t[:, T * r1 + 512 * tb:T * r1 + 512 * tb + 512],
                             start=True, stop=True, tile_position=(0, 64)).then_inc(p.sem["t"], 1)
                p.on("tensor", fn)
                mmt = p.bump("t")
                p.wait("scalar", "t", mmt)

                def fn(e, tb=tb, bank=bank, bias_name=bias_name, out_pref=out_pref):
                    e.activation(out_pref[:, 512 * tb:512 * tb + 512], bank[:],
                                 AF.Identity, bias=A[bias_name][:], scale=1.0).then_inc(p.sem["s"], 1)
                p.on("scalar", fn)
                st[("latbank_s", b, tb % 2)] = p.bump("s")
            return p.now("s")

        for pk in range(4):
            r0 = qkv_chunk(2 * pk)
            r1 = qkv_chunk(2 * pk + 1)
            p.wait("tensor", "v", st[("qkv_ev", b, 2 * pk + 1)])
            if pk > 0:
                p.wait("tensor", "v", st[("rope_q", b, pk - 1)])
            qlat_s = latent_mms(r0, r1, "q2l_st", "q2l_bb", A[f"latraw{b}"])
            qkv_consumed(r0, r1)
            if pk == 0:
                _ck(f"qlat0_{b}")
            p.wait("vector", "s", qlat_s)
            rq = rope_pack(A[f"latraw{b}"], A[f"qrope{b}"])
            st[("rope_q", b, pk)] = rq
            if pk == 0:
                _ck(f"rope0_{b}")
            p.wait("tensor", "v", rq)
            for tb in range(NTB):
                bank = PSB[tb % 2]
                prev = st.get(("latbank_s", b, tb % 2))
                if prev:
                    p.wait("tensor", "s", prev)

                def fn(e, tb=tb, bank=bank):
                    e.matmul(bank[:], A["wqk_st"][:],
                             A[f"qrope{b}"][:, 512 * tb:512 * tb + 512],
                             start=True, stop=True).then_inc(p.sem["t"], 1)
                p.on("tensor", fn)
                mmt = p.bump("t")
                p.wait("scalar", "t", mmt)

                def fn(e, tb=tb, bank=bank, pk=pk):
                    e.activation(A[f"qf{b}"][:, T * pk + 512 * tb:T * pk + 512 * tb + 512], bank[:],
                                 AF.Identity, bias=A["wqk_bb"][:], scale=1.0).then_inc(p.sem["s"], 1)
                p.on("scalar", fn)
                st[("latbank_s", b, tb % 2)] = p.bump("s")

            if pk == 0:
                _ck(f"wqk0_{b}")
            r0 = qkv_chunk(8 + 2 * pk)
            r1 = qkv_chunk(8 + 2 * pk + 1)
            p.wait("tensor", "v", st[("qkv_ev", b, 8 + 2 * pk + 1)])
            klat_s = latent_mms(r0, r1, "kv2l_st", "kv2l_bb", A[f"latraw{b}"])
            qkv_consumed(r0, r1)
            p.wait("vector", "s", klat_s)
            rk = rope_pack(A[f"latraw{b}"], A[f"kl{b}"][:, T * pk:T * pk + T])
            st[("rope_k", b, pk)] = rk
            if pk == 0:
                _ck(f"klat0_{b}")

        p.wait("tensor", "v", vones_v)
        for vc in range(8):
            ring = qkv_chunk(16 + vc)
            if vc == 0:
                _ck(f"vchunk_{b}")
            p.wait("tensor", "v", st[("qkv_ev", b, 16 + vc)])
            for tt2 in range(0, TT, 2):
                prev = st.get(("vbank_v", b))
                if prev:
                    p.wait("tensor", "v", prev)

                def fn(e, ring=ring, tt2=tt2):
                    for dt_ in range(min(2, TT - tt2)):
                        tt = tt2 + dt_
                        e.matmul(PSB[2][:, 32 * dt_:32 * dt_ + 32],
                                 qkv_t[0:64, T * ring + 128 * tt:T * ring + 128 * tt + 128],
                                 A["kv2l_wT"][0:64, :], start=True, stop=True)
                        mm = e.matmul(PSB[3][:, 32 * dt_:32 * dt_ + 32],
                                      qkv_t[64:128, T * ring + 128 * tt:T * ring + 128 * tt + 128],
                                      A["kv2l_wT"][64:128, :], start=True, stop=True, tile_position=(64, 0))
                    mm.then_inc(p.sem["t"], 1)
                p.on("tensor", fn)
                mmt = p.bump("t")
                if vc == 0 and tt2 == 0:
                    _ck(f"vmm_{b}")
                p.wait("vector", "t", mmt)

                def fn(e, vc=vc, tt2=tt2):
                    va = A[f"vaug{b}"][:]
                    nd = min(2, TT - tt2)
                    for hh in range(2):
                        h = 2 * vc + hh
                        dst = va[:, (h * NKT + tt2) * 33:(h * NKT + tt2) * 33 + 33 * nd].rearrange(
                            "p (k c) -> p k c", c=33)[:, :, 0:32]
                        src = PSB[2 + hh][:, 0:32 * nd].rearrange("p (k c) -> p k c", c=32)
                        sv_ = e.scalar_tensor_tensor(dst, src, 1.0, A["kvb_bc"][:].rearrange(
                            "p (k c) -> p k c", k=1).broadcast_to([128, nd, 32]),
                            op0=ALU.mult, op1=ALU.add)
                    sv_.then_inc(p.sem["v"], 1)
                p.on("vector", fn)
                st[("vbank_v", b)] = p.bump("v")
            qkv_consumed(ring)
            if vc == 0:
                _ck(f"v0_{b}")

        sbuf_free(f"qrope{b}", f"rsc{b}", f"latraw{b}", f"qkv{b}",
                  f"cawst{b}", f"cawld{b}", f"ln1fm{b}")
        _ck(f"qkv_{b}")

        sbuf_scoped(f"sqb{b}", [128, T], BF16)
        for which, srcn, dstn in (("q", f"qf{b}", "qn2"), ("k", f"kl{b}", "kn2")):
            for pk in range(4):
                dep_s = st.get(("sq_nm_t", b))
                if dep_s:
                    p.wait("scalar", "t", dep_s)
                if which == "k":
                    p.wait("scalar", "v", st[("rope_k", b, pk)])

                def fn(e, srcn=srcn, pk=pk):
                    e.activation(A[f"sqb{b}"][:], A[srcn][:, T * pk:T * pk + T], AF.Square).then_inc(p.sem["s"], 1)
                p.on("scalar", fn)
                sq_s = p.bump("s")
                p.wait("tensor", "s", sq_s)
                dep_v = st.get(("nm_red_v", b))
                if dep_v:
                    p.wait("tensor", "v", dep_v)

                def fn(e):
                    for tb in range(NTB):
                        mm = e.matmul(PSB[tb % 4][0:4, :], A["k_headones"][:],
                                      A[f"sqb{b}"][:, 512 * tb:512 * tb + 512], start=True, stop=True)
                    mm.then_inc(p.sem["t"], 1)
                p.on("tensor", fn)
                nm_t = p.bump("t")
                st[("sq_nm_t", b)] = nm_t
                p.wait("vector", "t", nm_t)

                def fn(e, pk=pk, dstn=dstn):
                    for tb in range(NTB):
                        e.reduce_max(A["chn"][0:4, tb:tb + 1], PSB[tb % 4][0:4, :], axis=AX.X)
                    e.drain()
                    e.reduce_max(A[dstn][0:4, pk:pk + 1], A["chn"][0:4, 0:NTB], axis=AX.X).then_inc(p.sem["v"], 1)
                p.on("vector", fn)
                st[("nm_red_v", b)] = p.bump("v")
        norms_v = p.now("v")
        sbuf_free(f"sqb{b}")
        p.wait("scalar", "v", norms_v)

        def fn(e):
            e.activation(A["qn2"][0:4, :], A["qn2"][0:4, :], AF.Sqrt)
            e.activation(A["kn2"][0:4, :], A["kn2"][0:4, :], AF.Sqrt).then_inc(p.sem["s"], 1)
        p.on("scalar", fn)
        sq2 = p.bump("s")
        p.wait("vector", "s", sq2)

        def fn(e):
            e.tensor_mul(A["negc"][0:4, :], A["qn2"][0:4, :], A["kn2"][0:4, :])
            e.drain()
            e.tensor_scalar(A["negc"][0:4, :], A["negc"][0:4, :], -float(ISQ), None,
                            op0=ALU.mult, op1=ALU.bypass)
            e.drain()
            nc3 = A["negc"][0:4, :].rearrange("p (k o) -> p k o", o=1).broadcast_to([4, 4, 4])
            e.tensor_mul(A["negcm"][0:4, :].rearrange("p (k o) -> p k o", o=4), nc3,
                         A["k_mask16"][0:4, :].rearrange("p (k o) -> p k o", o=4)).then_inc(p.sem["v"], 1)
        p.on("vector", fn)
        negc_v = p.bump("v")
        p.wait("tensor", "v", negc_v)

        def fn(e):
            e.matmul(PSB[0][0:128, 0:16], A["k_ones4x128"][0:4, :], A["negcm"][0:4, :],
                     start=True, stop=True).then_inc(p.sem["t"], 1)
        p.on("tensor", fn)
        negc_t = p.bump("t")
        p.wait("scalar", "t", negc_t)

        def fn(e):
            e.copy(A["negc_bc"][:], PSB[0][0:128, 0:16]).then_inc(p.sem["s"], 1)
        p.on("scalar", fn)
        negc_s = p.bump("s")
        _ck(f"ch_{b}")

        nonlocal psbf_a, psbf_b
        psum_free("psbf_b", "psbf_a")
        PSY[2] = psum_scoped("pb6", [128, 512], F32)
        PSY[3] = psum_scoped("pb7", [128, 512], F32)
        sbuf_scoped(f"et{b}", [128, 8 * 512], BF16)
        sbuf_scoped(f"ysb{b}", [128, T], BF16)
        sbuf_scoped(f"yn{b}", [128, 512], BF16)
        sbuf_scoped(f"Dt{b}", [128, T], F32)

        def fn(e):
            e.memset(A[f"Dt{b}"][:], 1.0).then_inc(p.sem["v"], 1)
        p.on("vector", fn)
        dt_v = p.bump("v")
        p.wait("tensor", "v", dt_v)
        p.wait("tensor", "s", negc_s)
        p.wait("scalar", "s", negc_s)

        for pk in range(4):
            prev = st.get(("selmm_t", b))
            if prev:
                p.wait("vector", "t", prev)
            for qb in range(NTB):
                nkt_q = 4 * qb + 4
                pend = []

                def emit_pv(kt, half, q0, nq, ets, pk=pk, qb=qb, nkt_q=nkt_q):
                    exp_s = st[("exp_done", b, pk, kt, half)]
                    mv = st.get(("mask_v", b, pk, kt % 2, half))
                    if kt - 4 * qb >= 0 and mv:
                        p.wait("tensor", "v", mv)
                    p.wait("tensor", "s", exp_s)

                    def fn(e, pk=pk, kt=kt, qb=qb, q0=q0, nq=nq, ets=ets, nkt_q=nkt_q, half=half):
                        for i in range(2):
                            hp = 2 * half + i
                            h = 4 * pk + hp
                            mm = e.matmul(PSY[hp][0:33, (q0 - 512 * qb):(q0 - 512 * qb) + nq],
                                          A[f"vaug{b}"][:, (h * NKT + kt) * 33:(h * NKT + kt) * 33 + 33],
                                          A[f"et{b}"][:, 512 * (ets + hp):512 * (ets + hp) + nq],
                                          start=(kt == 0), stop=(kt == nkt_q - 1))
                        mm.then_inc(p.sem["t"], 1)
                    p.on("tensor", fn)
                    pv_t = p.bump("t")
                    st[("pv_t", b, pk, kt % 2, half)] = pv_t
                    return pv_t

                for kt in range(nkt_q):
                    d = kt - 4 * qb
                    q0 = 512 * qb + (128 * d if d >= 0 else 0)
                    nq = 512 * (qb + 1) - q0
                    ets = (kt % 2) * 4
                    for half in (0, 1):
                        rnd = 2 * kt + half
                        bset = 2 * (rnd % 2)
                        prev = st.get(("exp_s", b, pk, rnd % 2))
                        if prev:
                            p.wait("tensor", "s", prev)

                        def fn(e, pk=pk, kt=kt, q0=q0, nq=nq, half=half, bset=bset):
                            for i in range(2):
                                hp = 2 * half + i
                                mm = e.matmul(PSB[bset + i][0:128, 0:nq],
                                              A[f"kl{b}"][32 * hp:32 * hp + 32, T * pk + 128 * kt:T * pk + 128 * kt + 128],
                                              A[f"qf{b}"][32 * hp:32 * hp + 32, T * pk + q0:T * pk + q0 + nq],
                                              start=True, stop=True, tile_position=(32 * hp, 0))
                            mm.then_inc(p.sem["t"], 1)
                        p.on("tensor", fn)
                        s_t = p.bump("t")
                        p.wait("scalar", "t", s_t)
                        prev = st.get(("pv_t", b, pk, kt % 2, half))
                        if prev:
                            p.wait("scalar", "t", prev)
                        prev = st.get(("mask_v", b, pk, kt % 2, half))
                        if prev:
                            p.wait("scalar", "v", prev)

                        def fn(e, pk=pk, ets=ets, nq=nq, half=half, bset=bset):
                            for i in range(2):
                                hp = 2 * half + i
                                h = 4 * pk + hp
                                e.activation(A[f"et{b}"][:, 512 * (ets + hp):512 * (ets + hp) + nq],
                                             PSB[bset + i][0:128, 0:nq], AF.Exp,
                                             bias=A["negc_bc"][:, h:h + 1], scale=ISQ).then_inc(p.sem["s"], 1)
                        p.on("scalar", fn)
                        for _ in range(2):
                            p.bump("s")
                        exp_s = p.now("s")
                        st[("exp_s", b, pk, rnd % 2)] = exp_s
                        st[("exp_done", b, pk, kt, half)] = exp_s
                        if d >= 0:
                            p.wait("vector", "s", exp_s)

                            def fn(e, ets=ets, half=half):
                                for i in range(2):
                                    hp = 2 * half + i
                                    sl = slice(512 * (ets + hp), 512 * (ets + hp) + 128)
                                    mmv = e.tensor_mul(A[f"et{b}"][:, sl], A[f"et{b}"][:, sl], A["k_diagmask"][:])
                                mmv.then_inc(p.sem["v"], 1)
                            p.on("vector", fn)
                            mask_v = p.bump("v")
                            st[("mask_v", b, pk, kt % 2, half)] = mask_v
                        pend.append((kt, half, q0, nq, ets))
                        if len(pend) > 1:
                            emit_pv(*pend.pop(0))
                for args in pend:
                    emit_pv(*args)
                pv_t = p.now("t")
                p.wait("vector", "t", pv_t)
                p.wait("scalar", "t", pv_t)

                def fn(e, pk=pk, qb=qb):
                    e.tensor_copy(A[f"Dt{b}"][32:33, 512 * qb:512 * qb + 512], PSY[0][32:33, :])
                    e.tensor_copy(A[f"Dt{b}"][64:65, 512 * qb:512 * qb + 512], PSY[1][32:33, :])
                    e.tensor_copy(A[f"Dt{b}"][96:97, 512 * qb:512 * qb + 512], PSY[2][32:33, :])
                    e.tensor_copy(A[f"Dt{b}"][0:1, 512 * qb:512 * qb + 512], PSY[3][32:33, :])
                    e.tensor_copy(A[f"ysb{b}"][0:32, 512 * qb:512 * qb + 512], PSY[0][0:32, :])
                    e.tensor_copy(A[f"ysb{b}"][32:64, 512 * qb:512 * qb + 512], PSY[1][0:32, :]).then_inc(p.sem["v"], 1)
                p.on("vector", fn)
                dcp_v = p.bump("v")

                def fn(e, pk=pk, qb=qb):
                    e.copy(A[f"ysb{b}"][64:96, 512 * qb:512 * qb + 512], PSY[2][0:32, :])
                    e.copy(A[f"ysb{b}"][96:128, 512 * qb:512 * qb + 512], PSY[3][0:32, :]).then_inc(p.sem["s"], 1)
                p.on("scalar", fn)
                ycp_s = p.bump("s")
                st[("yev", b, pk, qb)] = (dcp_v, ycp_s)
                p.wait("tensor", "v", dcp_v)
                p.wait("tensor", "s", ycp_s)
            p.wait("vector", "s", p.now("s"))
            p.wait("tensor", "v", p.now("v"))
            for qb in range(NTB):
                prev = st.get(("ynorm_v", b))
                if prev:
                    p.wait("tensor", "v", prev)

                def fn(e, qb=qb):
                    e.matmul(PSB[2][:], A["k_sel4"][:], A[f"Dt{b}"][:, 512 * qb:512 * qb + 512],
                             start=True, stop=True).then_inc(p.sem["t"], 1)
                p.on("tensor", fn)
                sel_t = p.bump("t")
                st[("selmm_t", b)] = sel_t
                p.wait("vector", "t", sel_t)

                def fn(e, qb=qb):
                    e.reciprocal(PSB[2][:], PSB[2][:])
                    e.drain()
                    e.tensor_mul(A[f"yn{b}"][:], A[f"ysb{b}"][:, 512 * qb:512 * qb + 512], PSB[2][:]).then_inc(p.sem["v"], 1)
                p.on("vector", fn)
                yn_v = p.bump("v")
                st[("ynorm_v", b)] = yn_v
                p.wait("tensor", "v", yn_v)
                prev = st.get(("yhead_s", b))
                if prev:
                    p.wait("tensor", "s", prev)

                def fn(e):
                    for hp in range(4):
                        ybank = PSB[4 + hp // 2]
                        mm = e.matmul(ybank[64 * (hp % 2):64 * (hp % 2) + 64, :],
                                      A["l2o_st"][32 * hp:32 * hp + 32, :],
                                      A[f"yn{b}"][32 * hp:32 * hp + 32, :],
                                      start=True, stop=True,
                                      tile_position=(32 * hp, 64 * (hp % 2)))
                    mm.then_inc(p.sem["t"], 1)
                p.on("tensor", fn)
                l2o_t = p.bump("t")
                p.wait("scalar", "t", l2o_t)

                def fn(e, pk=pk, qb=qb):
                    e.activation(A[f"yfm{b}"][:, (2 * pk) * T + 512 * qb:(2 * pk) * T + 512 * qb + 512],
                                 PSB[4][:], AF.Identity, bias=A["l2o_bb"][:], scale=1.0)
                    e.activation(A[f"yfm{b}"][:, (2 * pk + 1) * T + 512 * qb:(2 * pk + 1) * T + 512 * qb + 512],
                                 PSB[5][:], AF.Identity, bias=A["l2o_bb"][:], scale=1.0).then_inc(p.sem["s"], 1)
                p.on("scalar", fn)
                st[("yhead_s", b)] = p.bump("s")
        psum_free("pb7", "pb6")
        psbf_a = psum_scoped("psbf_a", [128, 1024], BF16)
        psbf_b = psum_scoped("psbf_b", [128, 1024], BF16)
        yfm_s = p.now("s")

        sbuf_free(f"Dt{b}", f"yn{b}", f"ysb{b}", f"et{b}", f"vaug{b}", f"kl{b}", f"qf{b}")
        _ck(f"sv_{b}")

        sbuf_scoped(f"cpld{b}", [128, 2 * C], BF16)
        sbuf_scoped(f"cpT{b}", [128, 8 * C], BF16)
        sbuf_scoped(f"attnf{b}", [128, 2 * C], F32)
        sbuf_scoped(f"xt2{b}", [128, 2 * C], F32)
        sbuf_scoped(f"xtl{b}", [128, 2 * C], F32)
        sbuf_scoped(f"xp{b}", [128, 2 * C], F32)
        sbuf_scoped(f"ln2tm{b}", [128, 2 * C], BF16)
        for cb in range(8):
            ld = cb % 2
            if cb >= 2:
                p.wait("gpsimd", "t", st[("cpT_t", b, cb - 2)])
            cpv = dma("gpsimd", A[f"cpld{b}"][:, C * ld:C * ld + C],
                      A["cproj_w"][128 * cb:128 * cb + 128, :], f"scp{ld}")
            p.wait("tensor", f"scp{ld}", cpv)
            if cb >= 1:
                p.wait("tensor", "v", st[("cpT_v", b, cb - 1)])

            def fn(e, ld=ld):
                src = A[f"cpld{b}"][:, C * ld:C * ld + C]
                for j in range(8):
                    mm = e.transpose(psbf_a[:, 128 * j:128 * j + 128], src[:, 128 * j:128 * j + 128], A["k_ident_bf"][:])
                mm.then_inc(p.sem["t"], 1)
            p.on("tensor", fn)
            st[("cpT_t", b, cb)] = p.bump("t")
            p.wait("vector", "t", st[("cpT_t", b, cb)])

            def fn(e, cb=cb):
                dst = A[f"cpT{b}"][:].rearrange("p (j c) -> p j c", j=8)[:, :, 128 * cb:128 * cb + 128]
                e.tensor_copy(dst, psbf_a[:].rearrange("p (j c) -> p j c", j=8)).then_inc(p.sem["v"], 1)
            p.on("vector", fn)
            st[("cpT_v", b, cb)] = p.bump("v")
        cpT_v = st[("cpT_v", b, 7)]
        p.wait("tensor", "v", cpT_v)
        p.wait("tensor", "s", yfm_s)

        for tt in range(TT):
            prev = st.get(("site1_v", b))
            if prev:
                p.wait("tensor", "v", prev)

            def fn(e, tt=tt):
                for j in range(8):
                    lhs = A[f"yfm{b}"][:, j * T + 128 * tt:j * T + 128 * tt + 128]
                    e.matmul(PSB[4][:], lhs, A[f"cpT{b}"][:, j * C:j * C + 512],
                             start=(j == 0), stop=(j == 7))
                    mm = e.matmul(PSB[5][:], lhs, A[f"cpT{b}"][:, j * C + 512:j * C + 1024],
                                  start=(j == 0), stop=(j == 7))
                mm.then_inc(p.sem["t"], 1)
            p.on("tensor", fn)
            cp_t = p.bump("t")
            slot = tt % 2
            af = A[f"attnf{b}"][:, C * slot:C * slot + C]
            p.wait("vector", "t", cp_t)

            def fn(e, af=af):
                e.tensor_add(af[:, 0:512], PSB[4][:], A["cpb_bc"][:, 0:512])
                e.tensor_add(af[:, 512:1024], PSB[5][:], A["cpb_bc"][:, 512:1024]).then_inc(p.sem["v"], 1)
            p.on("vector", fn)
            af_v = p.bump("v")
            st[("site1_v", b)] = af_v
            lslot, negms, rstd = ln_stats(af)
            xr = A[f"xt2{b}"][:, C * slot:C * slot + C]
            if tt >= 2:
                p.wait("sync", "s", st[("x2_use", b, tt - 2)])
            xv = dma("sync", xr, A["x"][b, 128 * tt:128 * tt + 128, :], f"sxr{slot}")
            p.wait("scalar", f"sxr{slot}", xv)
            if tt >= 2:
                p.wait("scalar", "v", st[("xp_v", b, tt - 2)])

            def fn(e, slot=slot, xr=xr, negms=negms):
                e.activation(A[f"xtl{b}"][:, C * slot:C * slot + C], xr, AF.Identity,
                             bias=negms, scale=1.0).then_inc(p.sem["s"], 1)
            p.on("scalar", fn)
            xtl_s = p.bump("s")
            st[("x2_use", b, tt)] = xtl_s
            ln_done(lslot)
            p.wait("vector", "s", xtl_s)
            if tt >= 2:
                p.wait("vector", f"sxp{slot}", st[("xp_dma", b, tt - 2)])
                p.wait("vector", "s", st[("ln2_s", b, tt - 2)])

            def fn(e, slot=slot, af=af, rstd=rstd):
                e.scalar_tensor_tensor(A[f"xp{b}"][:, C * slot:C * slot + C], af,
                                       rstd, A[f"xtl{b}"][:, C * slot:C * slot + C],
                                       op0=ALU.mult, op1=ALU.add).then_inc(p.sem["v"], 1)
            p.on("vector", fn)
            xp_v = p.bump("v")
            st[("xp_v", b, tt)] = xp_v
            p.wait("sync", "v", xp_v)
            st[("xp_dma", b, tt)] = dma("sync", A["xprime"][b, 128 * tt:128 * tt + 128, :],
                                        A[f"xp{b}"][:, C * slot:C * slot + C], f"sxp{slot}")
            lslot, negms, rstd = ln_stats(A[f"xp{b}"][:, C * slot:C * slot + C])
            if tt >= 2:
                p.wait("scalar", f"sl2{slot}", st[("ln2_dma", b, tt - 2)])

            def fn(e, slot=slot, negms=negms, rstd=rstd):
                e.activation(A[f"ln2tm{b}"][:, C * slot:C * slot + C],
                             A[f"xp{b}"][:, C * slot:C * slot + C], AF.Identity,
                             bias=negms, scale=rstd).then_inc(p.sem["s"], 1)
            p.on("scalar", fn)
            ln2_s = p.bump("s")
            ln_done(lslot)
            st[("ln2_s", b, tt)] = ln2_s
            p.wait("sync", "s", ln2_s)
            st[("ln2_dma", b, tt)] = dma("sync", A["ln2scr"][b, 128 * tt:128 * tt + 128, :],
                                         A[f"ln2tm{b}"][:, C * slot:C * slot + C], f"sl2{slot}")
        site1_end = p.snap()
        sbuf_free(f"ln2tm{b}", f"xp{b}", f"xtl{b}", f"xt2{b}", f"attnf{b}",
                  f"cpT{b}", f"cpld{b}", f"yfm{b}")
        return site1_end

    def gate(b):
        sbuf_scoped(f"g_ln2ld{b}", [128, 2 * C], BF16)
        sbuf_scoped(f"g_ln2fm{b}", [128, 8 * T], BF16)
        sbuf_scoped(f"gw1ld{b}", [128, 2 * C], BF16)
        sbuf_scoped(f"gw1T{b}", [128, 2 * 1024], BF16)
        sbuf_scoped(f"hfm{b}", [128, 8 * T], BF16)
        lnf = A[f"g_ln2fm{b}"]
        p.wait("sync", "sl20", p.now("sl20"))
        p.wait("sync", "sl21", p.now("sl21"))
        for tt in range(TT):
            slot = tt % 2
            if tt >= 2:
                p.wait("sync", "t", st[("g_lntr", b, tt - 2)])
            lv = dma("sync", A[f"g_ln2ld{b}"][:, C * slot:C * slot + C],
                     A["ln2scr"][b, 128 * tt:128 * tt + 128, :], f"sgl{slot}")
            p.wait("tensor", f"sgl{slot}", lv)
            if tt >= 1:
                p.wait("tensor", "v", st[("g_lnv", b, tt - 1)])

            def fn(e, slot=slot):
                src = A[f"g_ln2ld{b}"][:, C * slot:C * slot + C]
                for j in range(8):
                    mm = e.transpose(psbf_a[:, 128 * j:128 * j + 128], src[:, 128 * j:128 * j + 128], A["k_ident_bf"][:])
                mm.then_inc(p.sem["t"], 1)
            p.on("tensor", fn)
            st[("g_lntr", b, tt)] = p.bump("t")
            p.wait("vector", "t", st[("g_lntr", b, tt)])

            def fn(e, tt=tt):
                dst = lnf[:].rearrange("p (j t) -> p j t", j=8)[:, :, 128 * tt:128 * tt + 128]
                e.tensor_copy(dst, psbf_a[:].rearrange("p (j t) -> p j t", j=8)).then_inc(p.sem["v"], 1)
            p.on("vector", fn)
            st[("g_lnv", b, tt)] = p.bump("v")
        lnf_v = st[("g_lnv", b, TT - 1)]
        p.wait("tensor", "v", lnf_v)

        for oc in range(8):
            ld = oc % 2
            if oc >= 2:
                p.wait("gpsimd", "t", st[("gw1_t", b, oc - 2)])
            gv = dma("gpsimd", A[f"gw1ld{b}"][:, C * ld:C * ld + C],
                     A["gate_w1"][128 * oc:128 * oc + 128, :], f"sgw{ld}")
            p.wait("tensor", f"sgw{ld}", gv)
            if oc >= 1:
                p.wait("tensor", "v", st[("gw1_v", b, oc - 1)])

            def fn(e, ld=ld):
                src = A[f"gw1ld{b}"][:, C * ld:C * ld + C]
                for j in range(8):
                    mm = e.transpose(psbf_a[:, 128 * j:128 * j + 128], src[:, 128 * j:128 * j + 128], A["k_ident_bf"][:])
                mm.then_inc(p.sem["t"], 1)
            p.on("tensor", fn)
            st[("gw1_t", b, oc)] = p.bump("t")
            p.wait("vector", "t", st[("gw1_t", b, oc)])

            def fn(e, ld=ld):
                e.tensor_copy(A[f"gw1T{b}"][:, 1024 * ld:1024 * ld + 1024], psbf_a[:, 0:1024]).then_inc(p.sem["v"], 1)
            p.on("vector", fn)
            st[("gw1_v", b, oc)] = p.bump("v")
            p.wait("tensor", "v", st[("gw1_v", b, oc)])
            for tb in range(NTB):
                bank = PSB[4 + tb % 2]
                prev = st.get(("g1bank_s", b, tb % 2))
                if prev:
                    p.wait("tensor", "s", prev)

                def fn(e, ld=ld, tb=tb, bank=bank):
                    for j in range(8):
                        mm = e.matmul(bank[:],
                                      A[f"gw1T{b}"][:, 1024 * ld + 128 * j:1024 * ld + 128 * j + 128],
                                      lnf[:].rearrange("p (jj t) -> p jj t", jj=8)[:, j, 512 * tb:512 * tb + 512],
                                      start=(j == 0), stop=(j == 7))
                    mm.then_inc(p.sem["t"], 1)
                p.on("tensor", fn)
                g1_t = p.bump("t")
                p.wait("scalar", "t", g1_t)

                def fn(e, oc=oc, tb=tb, bank=bank):
                    e.activation(A[f"hfm{b}"][:, oc * T + 512 * tb:oc * T + 512 * tb + 512], bank[:],
                                 AF.Relu, bias=A["gb1c"][:, oc:oc + 1], scale=1.0).then_inc(p.sem["s"], 1)
                p.on("scalar", fn)
                st[("g1bank_s", b, tb % 2)] = p.bump("s")
        h_s = p.now("s")
        p.wait("tensor", "s", h_s)

        for tb in range(NTB):
            def fn(e, tb=tb):
                for j in range(8):
                    mm = e.matmul(PSB[0][0:8, :], A["gw2T"][:, 8 * j:8 * j + 8],
                                  A[f"hfm{b}"][:, j * T + 512 * tb:j * T + 512 * tb + 512],
                                  start=(j == 0), stop=(j == 7))
                mm.then_inc(p.sem["t"], 1)
            p.on("tensor", fn)
            g2_t = p.bump("t")
            p.wait("scalar", "t", g2_t)

            def fn(e, tb=tb):
                e.activation(A["g_scr"][0:8, :], PSB[0][0:8, :], AF.Identity,
                             accum_out=A["g_parts"][0:8, tb:tb + 1]).then_inc(p.sem["s"], 1)
            p.on("scalar", fn)
            g2_s = p.bump("s")
            p.wait("tensor", "s", g2_s)
        g2_all = p.now("s")
        p.wait("vector", "s", g2_all)

        def fn(e, b=b):
            e.reduce_sum(A["g32"][0:8, 2:3], A["g_parts"][0:8, 0:NTB], axis=AX.X)
            e.drain()
            e.tensor_scalar(A["g32"][0:8, b:b + 1], A["g32"][0:8, 2:3], 1.0 / T, A["gb2c"][0:8, :],
                            op0=ALU.mult, op1=ALU.add).then_inc(p.sem["v"], 1)
        p.on("vector", fn)
        gv_v = p.bump("v")
        p.wait("scalar", "v", gv_v)

        def fn(e, b=b):
            e.copy(A["gbf"][0:8, b:b + 1], A["g32"][0:8, b:b + 1]).then_inc(p.sem["s"], 1)
        p.on("scalar", fn)
        g_s = p.bump("s")
        sbuf_free(f"hfm{b}", f"gw1T{b}", f"gw1ld{b}", f"g_ln2fm{b}", f"g_ln2ld{b}")
        return g_s

    if upto == "p0":
        return _finish()
    try:
        attention(0)
    except _Stop:
        return _finish()
    p.barrier()
    if upto == "attn0":
        return _finish()
    try:
        gate(0)
    except _Stop:
        return _finish()
    p.barrier()
    if upto == "gate0":
        return _finish()
    try:
        attention(1)
    except _Stop:
        return _finish()
    p.barrier()
    g_done = gate(1)
    p.barrier()
    if upto == "gate1":
        return _finish()
    sbuf_free("k_sin_pack", "k_cos_pack")

    sbuf_scoped("tb1", [8, FF], BF16)
    sbuf_scoped("tb2", [8, C], BF16)
    sbuf_scoped("b2r", [2, C], F32)
    tbv = dma("gpsimd", A["tb1"][:], A["Tb1"][:], "sdmg")
    tbv = dma("gpsimd", A["tb2"][:], A["Tb2"][:], "sdmg")
    dma("sync", A["b2r"][0:1, :], A["b2"][:].rearrange("(a c) -> a c", a=1), "sdm")
    b2v = dma("sync", A["b2r"][1:2, :], A["b2"][:].rearrange("(a c) -> a c", a=1), "sdm")
    p.wait("tensor", "sdmg", tbv)
    p.wait("vector", "sdm", b2v)
    p.wait("scalar", "s", g_done)

    def fn(e):
        for j in range(16):
            cp = e.copy(A["gcols"][0:8, :].rearrange("p (o c) -> p c o", c=16)[:, j, :], A["gbf"][0:8, :])
        cp.then_inc(p.sem["s"], 1)
    p.on("scalar", fn)
    gcols_s = p.bump("s")
    p.wait("tensor", "s", gcols_s)

    def fn(e):
        e.matmul(PSB[0][0:128, 0:32], A["k_p_e"][0:8, :], A["gcols"][0:8, :], start=True, stop=True)
        for j in range(FF // 128):
            mm = e.matmul(PSB[1][0:128, 2 * j:2 * j + 2], A["tb1"][0:8, 128 * j:128 * j + 128],
                          A["gbf"][0:8, :], start=True, stop=True)
        e.matmul(PSB[2][0:2, :], A["gbf"][0:8, :], A["tb2"][0:8, 0:512], start=True, stop=True)
        mm = e.matmul(PSB[3][0:2, :], A["gbf"][0:8, :], A["tb2"][0:8, 512:1024], start=True, stop=True)
        mm.then_inc(p.sem["t"], 1)
    p.on("tensor", fn)
    mid_t = p.bump("t")
    p.wait("vector", "t", mid_t)

    def fn(e):
        e.tensor_mul(A["Gpat"][:], PSB[0][0:128, 0:32], A["k_m01"][:])
        b1b = A["b1T"][:].rearrange("p (j o) -> p j o", o=1).broadcast_to([128, FF // 128, 2])
        e.tensor_add(A["mb1T"][:].rearrange("p (j o) -> p j o", o=2), PSB[1][0:128, 0:2 * (FF // 128)].rearrange("p (j o) -> p j o", o=2), b1b)
        e.tensor_add(A["mb2row"][0:2, 0:512], PSB[2][0:2, :], A["b2r"][0:2, 0:512])
        e.tensor_add(A["mb2row"][0:2, 512:1024], PSB[3][0:2, :], A["b2r"][0:2, 512:1024]).then_inc(p.sem["v"], 1)
    p.on("vector", fn)
    mid_v = p.bump("v")
    p.wait("tensor", "v", mid_v)

    for b in range(2):
        def fn(e, b=b):
            e.matmul(PSB[4][:], A["k_sel2"][0:2, 128 * b:128 * b + 128],
                     A["mb2row"][0:2, 0:512], start=True, stop=True)
            e.matmul(PSB[5][:], A["k_sel2"][0:2, 128 * b:128 * b + 128],
                     A["mb2row"][0:2, 512:1024], start=True, stop=True).then_inc(p.sem["t"], 1)
        p.on("tensor", fn)
        mb2_t = p.bump("t")
        p.wait("scalar", "t", mb2_t)

        def fn(e, b=b):
            e.copy(A["mb2bc"][:, C * b:C * b + 512], PSB[4][:])
            e.copy(A["mb2bc"][:, C * b + 512:C * b + 1024], PSB[5][:]).then_inc(p.sem["s"], 1)
        p.on("scalar", fn)
        mb2_s = p.bump("s")
        p.wait("tensor", "s", mb2_s)
    p.barrier()
    sbuf_free("b2r", "tb2", "tb1")
    if upto == "mid":
        return _finish()

    sbuf_scoped("ln2fm", [128, 8 * 2 * T], BF16)
    sbuf_scoped("f_ln2ld", [128, 2 * C], BF16)
    _f_lnt = []
    _f_lnv = []
    p.wait("sync", "sl20", p.now("sl20"))
    p.wait("sync", "sl21", p.now("sl21"))
    for b in range(2):
        for tt in range(TT):
            slot = tt % 2
            if len(_f_lnt) >= 2:
                p.wait("sync", "t", _f_lnt[-2])
            lv = dma("sync", A["f_ln2ld"][:, C * slot:C * slot + C],
                     A["ln2scr"][b, 128 * tt:128 * tt + 128, :], f"sfl{slot}")
            p.wait("tensor", f"sfl{slot}", lv)
            if _f_lnv:
                p.wait("tensor", "v", _f_lnv[-1])

            def fn(e, slot=slot):
                src = A["f_ln2ld"][:, C * slot:C * slot + C]
                for j in range(8):
                    mm = e.transpose(psbf_a[:, 128 * j:128 * j + 128], src[:, 128 * j:128 * j + 128], A["k_ident_bf"][:])
                mm.then_inc(p.sem["t"], 1)
            p.on("tensor", fn)
            _f_lnt.append(p.bump("t"))
            p.wait("vector", "t", _f_lnt[-1])

            def fn(e, b=b, tt=tt):
                dst = A["ln2fm"][:].rearrange("p (j b2 t) -> p j b2 t", j=8, b2=2)[:, :, b, 128 * tt:128 * tt + 128]
                e.tensor_copy(dst, psbf_a[:].rearrange("p (j t) -> p j t", j=8)).then_inc(p.sem["v"], 1)
            p.on("vector", fn)
            _f_lnv.append(p.bump("v"))
    p.barrier()
    sbuf_free("f_ln2ld")
    if upto == "ln2fm":
        return _finish()

    sbuf_scoped("twld", [128, 16 * 512], BF16)
    sbuf_scoped("wrep", [128, 3 * 1024], F32)
    sbuf_scoped("mwfm", [128, 2 * 1024], BF16)
    sbuf_scoped("mw1Ts", [128, 8 * G64 * 128], BF16)
    sbuf_scoped("mw2Ts", [128, NFB * 16 * 128], BF16)
    sbuf_scoped("usup", [128, NFB * THW], BF16)
    sbuf_scoped("ffev", [128, 2 * C], F32)

    st["tw_grp"] = 0
    st["wrep_seq"] = 0
    for S in range(NSUP if upto != 'sup1' else 1):
        for mat in (1, 2):
            dst_sup = A["mw1Ts"] if mat == 1 else A["mw2Ts"]
            inner = C if mat == 1 else FSUP
            nib = inner // 512
            for grp in range(G64):
                gseq = st["tw_grp"]
                st["tw_grp"] = gseq + 1
                tbase = (gseq % 2) * 8
                wslot = st["wrep_seq"] % 3
                st["wrep_seq"] += 1
                mslot = grp % 2
                tw_v = None
                for j in range(4):
                    for ib in range(nib):
                        tws = tbase + (j * nib + ib) % 8
                        prev = st.get(("tw_used_t", tws))
                        if prev:
                            p.wait("gpsimd", "t", prev)
                        if mat == 1:
                            src = A["TW1"][:, S * FSUP + grp * 64 + 16 * j:S * FSUP + grp * 64 + 16 * j + 16,
                                           512 * ib:512 * ib + 512]
                        else:
                            src = A["TW2"][:, grp * 64 + 16 * j:grp * 64 + 16 * j + 16,
                                           S * FSUP + 512 * ib:S * FSUP + 512 * ib + 512]
                        tw_v = dma("gpsimd", A["twld"][:, 512 * tws:512 * tws + 512], src, f"stw{gseq % 2}")
                        st[("tw_slot_v", tws)] = tw_v
                prev = st.get(("wrep_used_v", wslot))
                if prev:
                    p.wait("sync", "v", prev)
                wv = None
                for j in range(4):
                    for bb in range(2):
                        if mat == 1:
                            src = A["W1"][S * FSUP + grp * 64 + 16 * j:S * FSUP + grp * 64 + 16 * j + 16, :]
                        else:
                            src = A["W2"][grp * 64 + 16 * j:grp * 64 + 16 * j + 16, S * FSUP:S * FSUP + FSUP]
                        wv = dma("sync", A["wrep"][32 * j + 16 * bb:32 * j + 16 * bb + 16,
                                                   1024 * wslot:1024 * wslot + 1024], src, f"swr{wslot}")
                prev = st.get(("emm_ev_v",))
                if prev:
                    p.wait("tensor", "v", prev)
                p.wait("tensor", f"stw{gseq % 2}", tw_v)

                def fn(e, nib=nib, tbase=tbase):
                    for ib in range(nib):
                        for j in range(4):
                            tws = tbase + (j * nib + ib) % 8
                            mm = e.matmul(PSB[4 + ib % 2][32 * j:32 * j + 32, :], A["Gpat"][:, 0:32],
                                          A["twld"][:, 512 * tws:512 * tws + 512],
                                          start=True, stop=True, tile_position=(0, 32 * j))
                    mm.then_inc(p.sem["t"], 1)
                p.on("tensor", fn)
                emm_t = p.bump("t")
                for k in range(8):
                    st[("tw_used_t", tbase + k)] = emm_t
                p.wait("vector", "t", emm_t)
                p.wait("vector", f"swr{wslot}", wv)
                prev = st.get(("mwfm_used_t", mslot))
                if prev:
                    p.wait("vector", "t", prev)

                def fn(e, nib=nib, wslot=wslot, mslot=mslot):
                    for ib in range(nib):
                        ta = e.tensor_add(A["mwfm"][:, 1024 * mslot + 512 * ib:1024 * mslot + 512 * ib + 512],
                                     PSB[4 + ib % 2][:], A["wrep"][:, 1024 * wslot + 512 * ib:1024 * wslot + 512 * ib + 512])
                    ta.then_inc(p.sem["v"], 1)
                p.on("vector", fn)
                mwfm_v = p.bump("v")
                st[("wrep_used_v", wslot)] = mwfm_v
                st[("emm_ev_v",)] = mwfm_v
                p.wait("tensor", "v", mwfm_v)
                prev = st.get(("mwT_ev_v",))
                if prev:
                    p.wait("tensor", "v", prev)

                def fn(e, mslot=mslot):
                    for sub in range(8):
                        mm = e.transpose(psbf_b[:, 128 * sub:128 * sub + 128],
                                         A["mwfm"][:, 1024 * mslot + 128 * sub:1024 * mslot + 128 * sub + 128],
                                         A["k_ident_bf"][:])
                    mm.then_inc(p.sem["t"], 1)
                p.on("tensor", fn)
                mwT_t = p.bump("t")
                st[("mwfm_used_t", mslot)] = mwT_t
                p.wait("vector", "t", mwT_t)

                def fn(e, grp=grp, dst_sup=dst_sup, mat=mat):
                    srcv = psbf_b[:].rearrange("p (s j bb i) -> p s bb j i", s=8, j=4, bb=2)
                    if mat == 1:
                        dst = dst_sup[:].rearrange(
                            "p (s fb bb gi i) -> p s fb bb gi i", s=8, fb=NFB, bb=2, gi=2)[
                            :, :, grp // 2, :, grp % 2, :].rearrange(
                            "p s bb (j i) -> p s bb j i", j=4)
                    else:
                        dst = dst_sup[:].rearrange(
                            "p (s bb g i) -> p s bb g i", s=8, bb=2, g=16)[
                            :, :, :, grp, :].rearrange(
                            "p s bb (j i) -> p s bb j i", j=4)
                    e.tensor_copy(dst, srcv).then_inc(p.sem["v"], 1)
                p.on("vector", fn)
                st[("mwT_ev_v",)] = p.bump("v")
        mwT_done = st[("mwT_ev_v",)]
        p.wait("tensor", "v", mwT_done)

        if S > 0:
            p.wait("gpsimd", "sa0", p.now("sa0"))
            p.wait("gpsimd", "sa1", p.now("sa1"))
        for b in range(2):
            for th in range(TH):
                for fb in range(NFB):
                    for tb in range(THW // 512):
                        bank = PSB[tb % 2]
                        prev = st.get(("u_ev_s", tb % 2))
                        if prev:
                            p.wait("tensor", "s", prev)

                        def fn(e, b=b, th=th, fb=fb, tb=tb, bank=bank):
                            for hs in range(8):
                                c0 = hs * (NFB * 256) + fb * 256 + b * 128
                                ap = A["mw1Ts"][:, c0:c0 + 128]
                                mm = e.matmul(bank[:], ap,
                                              A["ln2fm"][:].rearrange("p (j b2 t) -> p j b2 t", j=8, b2=2)[
                                                  :, hs, b, th * THW + 512 * tb:th * THW + 512 * tb + 512],
                                              start=(hs == 0), stop=(hs == 7))
                            mm.then_inc(p.sem["t"], 1)
                        p.on("tensor", fn)
                        mm1_t = p.bump("t")
                        p.wait("scalar", "t", mm1_t)

                        def fn(e, b=b, fb=fb, tb=tb, bank=bank, S=S):
                            col = 2 * (S * NFB + fb) + b
                            e.activation(A["usup"][:, fb * THW + 512 * tb:fb * THW + 512 * tb + 512], bank[:],
                                         AF.Gelu_apprx_tanh, bias=A["mb1T"][:, col:col + 1],
                                         scale=1.0).then_inc(p.sem["s"], 1)
                        p.on("scalar", fn)
                        st[("u_ev_s", tb % 2)] = p.bump("s")
                u_done = p.now("s")
                p.wait("tensor", "s", u_done)
                for t8 in range(THW // 128):
                    prev = st.get(("ffev_v",))
                    if prev:
                        p.wait("tensor", "v", prev)

                    def fn(e, b=b, th=th, t8=t8):
                        for fb in range(NFB):
                            lhs = A["usup"][:, fb * THW + 128 * t8:fb * THW + 128 * t8 + 128]
                            c0 = fb * 2048 + b * 1024
                            e.matmul(PSB[4][:], lhs, A["mw2Ts"][:, c0:c0 + 512],
                                     start=(fb == 0), stop=(fb == NFB - 1))
                            mm = e.matmul(PSB[5][:], lhs, A["mw2Ts"][:, c0 + 512:c0 + 1024],
                                          start=(fb == 0), stop=(fb == NFB - 1))
                        mm.then_inc(p.sem["t"], 1)
                    p.on("tensor", fn)
                    mm2_t = p.bump("t")
                    p.wait("vector", "t", mm2_t)
                    eslot = t8 % 2
                    preva = st.get(("acc_dma", t8 % 2))
                    if preva:
                        p.wait("vector", f"sa{t8 % 2}", preva)

                    def fn(e, eslot=eslot):
                        e.tensor_copy(A["ffev"][:, C * eslot:C * eslot + 512], PSB[4][:])
                        e.tensor_copy(A["ffev"][:, C * eslot + 512:C * eslot + 1024], PSB[5][:]).then_inc(p.sem["v"], 1)
                    p.on("vector", fn)
                    fe_v = p.bump("v")
                    st[("ffev_v",)] = fe_v
                    p.wait("gpsimd", "v", fe_v)
                    sem = p.ensure(f"sa{t8 % 2}")
                    row0 = th * THW + 128 * t8

                    def fn(e, b=b, row0=row0, eslot=eslot, S=S, sem=sem):
                        if S == 0:
                            e.dma_start(A["ffnacc"][b, row0:row0 + 128, :],
                                        A["ffev"][:, C * eslot:C * eslot + C]).then_inc(sem, 16)
                        else:
                            e.dma_start(A["ffnacc"][b, row0:row0 + 128, :],
                                        A["ffev"][:, C * eslot:C * eslot + C],
                                        accum_op=ALU.add).then_inc(sem, 16)
                    p.on("gpsimd", fn)
                    st[("acc_dma", t8 % 2)] = p.bump(f"sa{t8 % 2}", 16)
    p.barrier()
    if upto in ("ffn", "sup1"):
        return _finish()
    sbuf_free("ffev", "usup", "mw2Ts", "mw1Ts", "mwfm", "wrep", "twld",
              "ln2fm")

    sbuf_scoped("f_rd", [128, 2 * C], F32)
    sbuf_scoped("f_xp", [128, 2 * C], F32)
    sbuf_scoped("f_ff", [128, 2 * C], F32)
    sbuf_scoped("f_out", [128, 2 * C], F32)
    _fin_v = []
    _fin_s = []
    for b in range(2):
        for tt in range(TT):
            slot = tt % 2
            if len(_fin_v) >= 2:
                p.wait("sync", "v", _fin_v[-2])
            rv = dma("sync", A["f_rd"][:, C * slot:C * slot + C],
                     A["ffnacc"][b, 128 * tt:128 * tt + 128, :], f"sfr{slot}")
            xv = dma("sync", A["f_xp"][:, C * slot:C * slot + C],
                     A["xprime"][b, 128 * tt:128 * tt + 128, :], f"sfx{slot}")
            p.wait("vector", f"sfr{slot}", rv)
            p.wait("vector", f"sfx{slot}", xv)
            if len(_fin_s) >= 2:
                p.wait("vector", "s", _fin_s[-2])

            def fn(e, b=b, slot=slot):
                e.tensor_add(A["f_ff"][:, C * slot:C * slot + C],
                             A["f_rd"][:, C * slot:C * slot + C], A["mb2bc"][:, C * b:C * b + C]).then_inc(p.sem["v"], 1)
            p.on("vector", fn)
            ff_v = p.bump("v")
            lslot, negms, rstd = ln_stats(A["f_ff"][:, C * slot:C * slot + C])
            p.wait("scalar", f"sfx{slot}", xv)

            def fn(e, slot=slot, negms=negms):
                e.activation(A["f_xp"][:, C * slot:C * slot + C], A["f_xp"][:, C * slot:C * slot + C],
                             AF.Identity, bias=negms, scale=1.0).then_inc(p.sem["s"], 1)
            p.on("scalar", fn)
            fs = p.bump("s")
            ln_done(lslot)
            _fin_s.append(fs)
            p.wait("vector", "s", fs)
            if len(_fin_out := st.setdefault("_fin_out", [])) >= 2:
                p.wait("vector", f"so{slot}", _fin_out[-2])

            def fn(e, slot=slot, rstd=rstd):
                e.scalar_tensor_tensor(A["f_out"][:, C * slot:C * slot + C],
                                       A["f_ff"][:, C * slot:C * slot + C],
                                       rstd, A["f_xp"][:, C * slot:C * slot + C],
                                       op0=ALU.mult, op1=ALU.add).then_inc(p.sem["v"], 1)
            p.on("vector", fn)
            ov = p.bump("v")
            _fin_v.append(ov)
            p.wait("sync", "v", ov)
            st["_fin_out"].append(dma("sync", A["out"][b, 128 * tt:128 * tt + 128, :],
                                      A["f_out"][:, C * slot:C * slot + C], f"so{slot}"))
    p.barrier()

    p.emit()
    nc.finalize()
    return nc


_NC_CACHE = {}


def get_nc(T):
    if T not in _NC_CACHE:
        _NC_CACHE[T] = build_nc(T)
    return _NC_CACHE[T]


def kernel(**inputs):
    T = inputs["L0_attn_x"].shape[1]
    nc = get_nc(T)
    consts = make_consts(T)
    x = np.ascontiguousarray(np.asarray(inputs["L0_attn_x"], dtype=np.float32))
    nb = x.shape[0]
    ncores = nb // BC
    weights = {}
    name_map = {
        "c_attn_w": "L0_attn_c_attn_w", "c_attn_b": "L0_attn_c_attn_b",
        "q2l_w": "L0_attn_q2l_w", "q2l_b": "L0_attn_q2l_b",
        "kv2l_w": "L0_attn_kv2l_w", "kv2l_b": "L0_attn_kv2l_b",
        "l2o_w": "L0_attn_l2o_w", "l2o_b": "L0_attn_l2o_b",
        "wqk_w": "L0_attn_wqk_w", "wqk_b": "L0_attn_wqk_b",
        "cproj_w": "L0_attn_cproj_w", "cproj_b": "L0_attn_cproj_b",
        "gate_w1": "L0_ffn_gate_w1", "gate_b1": "L0_ffn_gate_b1",
        "gate_w2": "L0_ffn_gate_w2", "gate_b2": "L0_ffn_gate_b2",
        "W1": "L0_ffn_W1", "b1": "L0_ffn_b1",
        "W2": "L0_ffn_W2", "b2": "L0_ffn_b2",
        "TW1": "L0_ffn_TW1", "Tb1": "L0_ffn_Tb1",
        "TW2": "L0_ffn_TW2", "Tb2": "L0_ffn_Tb2",
    }
    for k, src in name_map.items():
        weights[k] = np.ascontiguousarray(np.asarray(inputs[src], dtype=np.float32))
    in_maps = []
    for c in range(ncores):
        m = {"x": x[BC * c:BC * c + BC]}
        m.update(weights)
        m.update(consts)
        in_maps.append(m)
    res = run_bass_kernel_spmd(nc, in_maps, core_ids=list(range(ncores)))
    out = np.concatenate([np.asarray(r["out"], dtype=np.float32) for r in res.results], axis=0)
    return out
